# version 15
# baseline (speedup 1.0000x reference)
import numpy as np
from contextlib import ExitStack

import concourse.bass as bass
import concourse.mybir as mybir
from concourse.bass_utils import run_bass_kernel_spmd

F32 = mybir.dt.float32
BF16 = mybir.dt.bfloat16
I16 = mybir.dt.int16
OP = mybir.AluOpType
AF = mybir.ActivationFunctionType
AX = mybir.AxisListType.X

T, C, H = 4096, 1024, 4096
NG = C // 128
NH = H // 128
TT = 256
NCH = T // TT
NTAU = TT // 128
QB = 127.0
EPS = 1e-5
KCLAMP = 60.0
LN127 = float(np.log(127.0))

PUMP_FRONTS = True
F0_HOLD = ()
N_PUMP = 1
EARLY_IN0 = False
PARTS = "full"


class _Sem:
    def __init__(self, h):
        self.h = h
        self.n = 0


class _Prog:
    def __init__(self, nc, ctx):
        self.nc = nc
        self.streams = {k: [] for k in ("tensor", "vector", "scalar", "sync", "gpsimd")}
        self.ctx = ctx

    def sem(self, name):
        return _Sem(self.ctx.enter_context(self.nc.semaphore(name)))

    def step(self, eng, fn, waits=(), inc=None, amt=1):
        wl = [(w[0].h, w[1]) for w in waits if w is not None]
        tok = None
        if inc is not None:
            inc.n += amt
            tok = (inc, inc.n)

        def run(e, wl=wl, fn=fn, inc=inc, amt=amt):
            for (h, v) in wl:
                e.wait_ge(h, v)
            last = fn(e)
            if inc is not None:
                assert last is not None, "step with inc must return last instruction"
                last.then_inc(inc.h, amt)
        self.streams[eng].append(run)
        return tok

    def dma(self, eng, pairs, sem, waits=()):
        wl = [(w[0].h, w[1]) for w in waits if w is not None]
        sem.n += 16 * len(pairs)
        tok = (sem, sem.n)

        def run(e, wl=wl, pairs=pairs, sem=sem):
            for (h, v) in wl:
                e.wait_ge(h, v)
            for (dst, src) in pairs:
                e.dma_start(dst, src).then_inc(sem.h, 16)
        self.streams[eng].append(run)
        return tok


def build_program(parts="full"):
    nc = bass.Bass("TRN2", target_bir_lowering=False, detect_race_conditions=False)

    x_in = nc.dram_tensor("x", [T, C], F32, kind="ExternalInput").ap()
    td_in = nc.dram_tensor("td", [C], F32, kind="ExternalInput").ap()
    tf_in = nc.dram_tensor("tf", [C], F32, kind="ExternalInput").ap()
    tm_in = nc.dram_tensor("tm", [C], F32, kind="ExternalInput").ap()
    cm_in = nc.dram_tensor("cm", [C], F32, kind="ExternalInput").ap()
    wk_in = nc.dram_tensor("wk", [C, C], F32, kind="ExternalInput").ap()
    wv_in = nc.dram_tensor("wv", [C, C], F32, kind="ExternalInput").ap()
    wr_in = nc.dram_tensor("wr", [C, C], F32, kind="ExternalInput").ap()
    wo_in = nc.dram_tensor("wo", [C, C], F32, kind="ExternalInput").ap()
    w1_in = nc.dram_tensor("w1", [H, C], F32, kind="ExternalInput").ap()
    b1_in = nc.dram_tensor("b1", [H], F32, kind="ExternalInput").ap()
    w2_in = nc.dram_tensor("w2", [C, H], F32, kind="ExternalInput").ap()
    b2_in = nc.dram_tensor("b2", [C], F32, kind="ExternalInput").ap()
    y_out = nc.dram_tensor("y", [T, C], F32, kind="ExternalOutput").ap()

    w1t = nc.dram_tensor("w1t", [128, 8, 8, 512], BF16, kind="Internal").ap()
    w2t = nc.dram_tensor("w2t", [128, NH, C], BF16, kind="Internal").ap()
    wkt = nc.dram_tensor("wkt", [128, NG, C], BF16, kind="Internal").ap()
    wvt = nc.dram_tensor("wvt", [128, NG, C], BF16, kind="Internal").ap()
    wrt = nc.dram_tensor("wrt", [128, NG, C], BF16, kind="Internal").ap()
    wot = nc.dram_tensor("wot", [128, NG, C], BF16, kind="Internal").ap()

    with ExitStack() as ctx:
        p = _Prog(nc, ctx)
        sb = lambda name, shape, dt=F32: ctx.enter_context(nc.sbuf_tensor(name, shape, dt))
        ps_ = lambda name, shape, dt=F32: ctx.enter_context(nc.psum_tensor(name, shape, dt))

        W8 = sb("W8", [128, NG]); EWN = sb("EWN", [128, NG]); U8E = sb("U8E", [128, NG])
        AM = sb("AM", [128, NG]); BM = sb("BM", [128, NG]); CM8 = sb("CM8", [128, NG])
        B2H = sb("B2H", [128, NG])
        VG = sb("VG", [32, 128])
        CEPS = sb("CEPS", [128, 1]); CL127 = sb("CL127", [128, 1]); C128 = sb("C128", [128, 1])
        LS1 = sb("LS1", [128, 1]); LB2S = sb("LB2S", [128, 1])
        RS1 = sb("RS1", [128, 1]); RS2 = sb("RS2", [128, 1])
        TH1 = sb("TH1", [128, 1]); TH2 = sb("TH2", [128, 1])
        RSC1 = sb("RSC1", [128, 1]); RSC2 = sb("RSC2", [128, 1])
        EXL = sb("EXL", [128, 2])
        SACC = sb("SACC", [128, 1]); SRED = sb("SRED", [128, 1])
        ONESF = sb("ONESF", [128, 128]); ONESB = sb("ONESB", [128, 128], BF16)
        IDF = sb("IDF", [128, 128]); IDB = sb("IDB", [128, 128], BF16)
        B1F = sb("B1F", [128, NH])

        XB = [sb(f"XB{i}", [128, NG, TT]) for i in range(3)]
        TSG = sb("TSG", [128, C])
        LNB = sb("LNB", [128, NG, TT])
        LN2 = [sb(f"LN2_{i}", [128, NG, TT], BF16) for i in range(3)]
        HAL = sb("HAL", [128, NG, 1]); ZCOL = sb("ZCOL", [128, NG, 1])
        XM = sb("XM", [128, NG, TT], BF16)
        XMF = sb("XMF", [128, NG, TT], BF16)
        XQY = sb("XQY", [128, NG, TT], BF16)
        YB = sb("YB", [128, NG, TT], BF16)
        ABSW = sb("ABSW", [128, NG, TT])
        CINT = sb("CINT", [128, 2048], I16)
        KT = sb("KT", [128, NG, TT]); VT = sb("VT", [128, NG, TT])
        GT = sb("GT", [128, 4, TT])
        TMP1 = sb("TMP1", [128, 4, TT]); TMP2 = sb("TMP2", [128, 4, TT])
        MT = sb("MT", [128, 4, TT + 1]); AT = sb("AT", [128, 4, TT + 1])
        BT = sb("BT", [128, 4, TT + 1])
        CARM = sb("CARM", [128, NG]); CARA = sb("CARA", [128, NG]); CARB = sb("CARB", [128, NG])
        HTM = [sb(f"HTM{i}", [128, H]) for i in range(2)]
        PSTG = sb("PSTG", [128, 3, H], BF16)
        XQHF = sb("XQHF", [128, 2, TT], BF16)
        STRB = sb("STRB", [128, 512], BF16)
        W1S = [sb(f"W1S{i}", [128, NG, 512], BF16) for i in range(2)]
        W2S = [sb(f"W2S{i}", [128, C], BF16) for i in range(2)]
        WKS = [sb(f"WKS{i}", [128, C], BF16) for i in range(3)]
        MS = sb("MS", [128, TT]); RS = sb("RS", [128, TT])
        XSC = sb("XSC", [128, 2]); LXS = sb("LXS", [128, 2])
        QCOL = sb("QCOL", [128, 2]); ACOL = sb("ACOL", [128, 2])
        HSC = sb("HSC", [128, 2]); QH = sb("QH", [128, 2]); BCOL = sb("BCOL", [128, 2])
        QR1 = sb("QR1", [1, TT])
        IRROW = sb("IRROW", [1, TT], BF16)
        BETA = sb("BETA", [128, TT])

        PM1 = ps_("PM1", [128, 1024])
        PKVR = ps_("PKVR", [128, 2048])
        PTR = ps_("PTR", [128, 512])
        PTB = ps_("PTB", [128, 1024], BF16)

        sT = p.sem("sT"); sV = p.sem("sV"); sS = p.sem("sS"); sG = p.sem("sG")
        sW = p.sem("sW"); sI = p.sem("sI"); sO = p.sem("sO")
        sBV = p.sem("sBV"); sBS = p.sem("sBS")

        def vbar(e, inst):
            sBV.n += 1
            inst.then_inc(sBV.h, 1)
            e.wait_ge(sBV.h, sBV.n)

        def sbar(e, inst):
            sBS.n += 1
            inst.then_inc(sBS.h, 1)
            e.wait_ge(sBS.h, sBS.n)

        def v3(h, free_total, n=NG, inner=TT, off=0):
            return bass.AP(h, off, [[free_total, 128], [inner, n], [1, inner]])

        def bc_row(h, inner=TT, n=NG, stride=None, off=0):
            return bass.AP(h, off, [[stride if stride is not None else inner, 128],
                                    [0, n], [1, inner]])

        def bc_col(h, n=NG, inner=TT):
            return bass.AP(h, 0, [[n, 128], [1, n], [0, inner]])

        def flat2(h, ncols, free_total, off=0):
            return bass.AP(h, off, [[free_total, 128], [1, ncols]])

        def g_init(e):
            e.memset(ONESF[:], 1.0)
            e.memset(ONESB[:], 1.0)
            e.memset(IDF[:], 0.0)
            e.affine_select(out=IDF[:], in_=IDF[:], compare_op=OP.not_equal,
                            fill=1.0, base=0, pattern=[[-1, 128]], channel_multiplier=1)
            e.memset(IDB[:], 0.0)
            e.affine_select(out=IDB[:], in_=IDB[:], compare_op=OP.not_equal,
                            fill=1.0, base=0, pattern=[[-1, 128]], channel_multiplier=1)
            e.memset(CEPS[:], EPS)
            e.memset(CL127[:], LN127)
            e.memset(C128[:], 128.0)
            e.memset(ZCOL[:], 0.0)
            e.memset(HAL[:], 0.0)
            e.memset(CARA[:], 0.0)
            e.memset(CARB[:], 0.0)
            return e.memset(CARM[:], -1e38)
        tok_ginit = p.step("gpsimd", g_init, inc=sG)

        tok_b1r = p.dma("gpsimd", [(PSTG[0:1, 2, 0:H], b1_in[None, :])], sI,
                        waits=[tok_ginit])

        vecs = [(td_in, W8), (tf_in, U8E), (tm_in, AM), (cm_in, CM8), (b2_in, B2H)]
        prev_tok = tok_ginit
        for (vin, vdst) in vecs:
            tok_l = p.dma("sync", [(VG[0:8, :], vin.rearrange("(g p) -> g p", p=128))],
                          sW, waits=[prev_tok])
            tok_t = p.step("tensor", lambda e: e.transpose(PTR[:, 0:8], VG[0:8, :], IDF[0:8, 0:8]),
                           waits=[tok_l, tok_ginit, prev_tok], inc=sT)
            prev_tok = p.step("vector", lambda e, vd=vdst: e.tensor_copy(vd[:], PTR[:, 0:8]),
                              waits=[tok_t], inc=sV)
        tok_params = prev_tok

        tok_b1l = p.dma("sync", [(VG[0:32, :], b1_in.rearrange("(g p) -> g p", p=128))],
                        sW, waits=[tok_params])
        tok_b1t = p.step("tensor", lambda e: e.transpose(PTR[:, 0:32], VG[0:32, :], IDF[0:32, 0:32]),
                         waits=[tok_b1l], inc=sT)
        tok_b1 = p.step("vector", lambda e: e.tensor_copy(B1F[:], PTR[:, 0:32]),
                        waits=[tok_b1t], inc=sV)

        def v_derived(e):
            e.tensor_scalar(W8[:], W8[:], 1.0 / T, None, OP.mult)
            vbar(e, e.tensor_scalar(BM[:], AM[:], -1.0, 1.0, OP.mult, OP.add))
            e.tensor_tensor(BM[:, 0:4], BM[:, 0:4], CM8[:, 0:4], OP.add)
            return e.tensor_scalar(B2H[:], B2H[:], 0.5, None, OP.mult)
        tok_vd = p.step("vector", v_derived, waits=[tok_params, tok_b1], inc=sV)

        def s_derived(e):
            e.activation(EWN[:], W8[:], AF.Exp, scale=-1.0)
            return e.activation(U8E[:], U8E[:], AF.Exp, scale=1.0 / T)
        tok_sd = p.step("scalar", s_derived, waits=[tok_vd], inc=sS)
        tok_base = tok_sd

        def q_round(e, vp, ci, gb):
            e.tensor_copy(ci, vp)
            e.scalar_tensor_tensor(gb, vp, -0.5, ci, OP.add, OP.is_gt)
            return e.scalar_tensor_tensor(vp, ci, -128.0, gb, OP.add, OP.add)

        last = {
            "pm1_free": tok_base, "pkvr_free": tok_base,
            "ptr_free": tok_b1, "ptb_free": tok_base,
            "xb": [tok_base, tok_base, tok_base], "tsg_free": tok_base,
            "w1s": [tok_base, tok_base], "w2s": [tok_base, tok_base],
            "wks": [tok_base, tok_base, tok_base], "scan": tok_base,
            "sxs": None, "sh": None, "ln_busy": tok_base, "pstg_free": tok_base,
            "s1t": tok_base, "s2t": tok_base, "b1r": tok_b1r,
            "w1t": tok_base, "w2t": tok_base, "wkvrt": tok_base, "wot": tok_base,
        }

        def KHf(q):
            return flat2(KT if q < 2 else VT, C, NG * TT, off=(q % 2) * C)

        def KH2f(b):
            return flat2(KT if b == 0 else VT, 2 * C, NG * TT)

        def gen_prologue():
            tok_zacc = p.step("vector", lambda e: e.memset(SACC[:], 0.0), inc=sV)
            rtoks = [tok_zacc] * 4
            for t in range(32):
                q = t % 4
                tok_l = p.dma("sync", [(KHf(q), w1_in[t * 128:(t + 1) * 128, :])],
                              sW, waits=[rtoks[q]])

                def v_abs(e, q=q):
                    vbar(e, e.tensor_reduce(SRED[:, 0:1], KHf(q), AX,
                                            OP.add, apply_absolute_value=True))
                    vbar(e, e.tensor_tensor(SACC[:], SACC[:], SRED[:], OP.add))
                    return e.tensor_copy(SACC[:], SACC[:])
                rtoks[q] = p.step("vector", v_abs, waits=[tok_l], inc=sV)
                if t % 8 == 7:
                    yield
            tok_s1mm = p.step("tensor", lambda e: e.matmul(PKVR[:, 0:1], ONESF[:],
                                                           SACC[:, 0:1],
                                                           start=True, stop=True),
                              waits=[rtoks[3], last["pkvr_free"]], inc=sT)

            def v_s1(e):
                vbar(e, e.tensor_scalar(SRED[:], PKVR[:, 0:1], 1.0 / (H * C), None, OP.mult))
                vbar(e, e.tensor_scalar_max(SRED[:], SRED[:], EPS))
                e.tensor_scalar(RSC1[:], SRED[:], 1.0 / 127.0, None, OP.mult)
                return e.tensor_scalar(TH1[:], SRED[:], 0.5, None, OP.mult)
            tok_vs1 = p.step("vector", v_s1, waits=[tok_s1mm], inc=sV)
            last["pkvr_free"] = tok_vs1

            def s_s1(e):
                e.activation(LS1[:], SRED[:], AF.Ln, scale=1.0 / 127.0)
                sbar(e, e.activation(RS1[:], SRED[:], AF.Ln))
                return e.activation(RS1[:], RS1[:], AF.Exp, scale=-1.0)
            tok_ss1 = p.step("scalar", s_s1, waits=[tok_vs1], inc=sS)
            last["s1t"] = tok_ss1
            yield "s1_done"

            tB_done = tok_ss1
            prev_dma = None
            for hs in range(8):
                tok_l = p.dma("sync", [(KHf(i), w1_in[(hs * 4 + i) * 128:
                                                      (hs * 4 + i + 1) * 128, :])
                                       for i in range(4)], sW, waits=[tB_done])

                def v_q(e):
                    thb = bass.AP(TH1, 0, [[1, 128], [0, C]])
                    for i in range(4):
                        vp = KHf(i)
                        gb = flat2(GT, C, 4 * TT)
                        e.scalar_tensor_tensor(gb, vp, 1.0, thb, OP.mult, OP.is_gt)
                        e.scalar_tensor_tensor(vp, vp, -1.0, thb, OP.mult, OP.is_gt)
                        last_i = e.tensor_tensor(PSTG[:, 0, i * C:(i + 1) * C],
                                                 gb, vp, OP.subtract)
                    return last_i
                tok_vq = p.step("vector", v_q, waits=[tok_l, tok_ss1], inc=sV)

                for j in range(8):
                    def t_tr(e, j=j):
                        for i in range(4):
                            last_i = e.transpose(PTB[:, i * 128:(i + 1) * 128],
                                                 PSTG[:, 0, i * C + j * 128: i * C + (j + 1) * 128],
                                                 IDB[:])
                        return last_i
                    tok_tr = p.step("tensor", t_tr, waits=[tok_vq, tok_ginit, prev_dma], inc=sT)
                    tok_ev = p.step("vector", lambda e: e.tensor_copy(STRB[:, 0:512], PTB[:, 0:512]),
                                    waits=[tok_tr, prev_dma], inc=sV)
                    prev_dma = p.dma("sync", [(w1t[:, hs, j, :], STRB[:, 0:512])], sW,
                                     waits=[tok_ev])
                tB_done = prev_dma
                yield
            last["w1t"] = prev_dma
            yield "w1_done"

            tok_zacc2 = p.step("vector", lambda e: e.memset(SACC[:], 0.0),
                               waits=[tB_done, tok_ss1, tok_s1mm], inc=sV)
            rtoks2 = [tok_zacc2, tok_zacc2]
            for it in range(16):
                bi = it % 2
                tok_l = p.dma("sync", [(KH2f(bi), w2_in.rearrange("d (a h) -> d a h", a=2)
                                        [it // 2 * 128:(it // 2 + 1) * 128, it % 2, :])],
                              sW, waits=[rtoks2[bi]])

                def v_abs2(e, bi=bi):
                    vbar(e, e.tensor_reduce(SRED[:, 0:1], KH2f(bi), AX, OP.add,
                                            apply_absolute_value=True))
                    vbar(e, e.tensor_tensor(SACC[:], SACC[:], SRED[:], OP.add))
                    return e.tensor_copy(SACC[:], SACC[:])
                rtoks2[bi] = p.step("vector", v_abs2, waits=[tok_l], inc=sV)
                if it % 4 == 3:
                    yield
            tok_s2mm = p.step("tensor", lambda e: e.matmul(PKVR[:, 0:1], ONESF[:],
                                                           SACC[:, 0:1],
                                                           start=True, stop=True),
                              waits=[rtoks2[0], rtoks2[1], last["pkvr_free"]], inc=sT)

            def v_s2(e):
                vbar(e, e.tensor_scalar(SRED[:], PKVR[:, 0:1], 1.0 / (H * C), None, OP.mult))
                vbar(e, e.tensor_scalar_max(SRED[:], SRED[:], EPS))
                e.tensor_scalar(RSC2[:], SRED[:], 0.5 / 127.0, None, OP.mult)
                return e.tensor_scalar(TH2[:], SRED[:], 0.5, None, OP.mult)
            tok_vs2 = p.step("vector", v_s2, waits=[tok_s2mm], inc=sV)
            last["pkvr_free"] = tok_vs2

            def s_s2(e):
                e.activation(LB2S[:], SRED[:], AF.Ln, scale=0.5 / 127.0)
                sbar(e, e.activation(RS2[:], SRED[:], AF.Ln))
                return e.activation(RS2[:], RS2[:], AF.Exp, scale=-1.0)
            tok_ss2 = p.step("scalar", s_s2, waits=[tok_vs2], inc=sS)
            last["s2t"] = tok_ss2
            yield "s2_done"

            tB2 = tok_ss2
            for quar in range(4):
                for i in range(2):
                    dti = quar * 2 + i
                    tok_l = p.dma("sync", [(KH2f(a), w2_in.rearrange("d (a h) -> d a h", a=2)
                                            [dti * 128:(dti + 1) * 128, a, :])
                                           for a in range(2)],
                                  sW, waits=[tB2])

                    def v_q2(e, i=i):
                        thb = bass.AP(TH2, 0, [[1, 128], [0, C]])
                        for q in range(4):
                            vp = KHf(q)
                            gb = flat2(GT, C, 4 * TT)
                            e.scalar_tensor_tensor(gb, vp, 1.0, thb, OP.mult, OP.is_gt)
                            e.scalar_tensor_tensor(vp, vp, -1.0, thb, OP.mult, OP.is_gt)
                            last_i = e.tensor_tensor(PSTG[:, i, q * C:(q + 1) * C],
                                                     gb, vp, OP.subtract)
                        return last_i
                    tB2 = p.step("vector", v_q2, waits=[tok_l, tok_ss2], inc=sV)
                for m in range(NH):
                    def t_tr2(e, m=m):
                        for i in range(2):
                            last_i = e.transpose(PTB[:, i * 128:(i + 1) * 128],
                                                 PSTG[:, i, m * 128:(m + 1) * 128], IDB[:])
                        return last_i
                    tok_tr = p.step("tensor", t_tr2, waits=[tB2, prev_dma], inc=sT)
                    tok_ev = p.step("vector", lambda e: e.tensor_copy(STRB[:, 0:256], PTB[:, 0:256]),
                                    waits=[tok_tr, prev_dma], inc=sV)
                    prev_dma = p.dma("sync", [(w2t[:, m, quar * 256:(quar + 1) * 256],
                                               STRB[:, 0:256])], sW, waits=[tok_ev])
                    if m % 8 == 7:
                        yield
                tB2 = prev_dma
            last["w2t"] = prev_dma

            tokAW = tB2
            final_tr = None
            for (win, wdst) in [(wk_in, wkt), (wv_in, wvt), (wr_in, wrt), (wo_in, wot)]:
                htoks = [tokAW, tokAW]
                ctoks = []
                for i in range(8):
                    bi = i % 2
                    tok_l = p.dma("sync", [(KHf(bi * 2), win[i * 128:(i + 1) * 128, :])], sW,
                                  waits=[htoks[bi]])
                    htoks[bi] = p.step("vector",
                                       lambda e, i=i, bi=bi: e.tensor_copy(
                                           PSTG[:, i // 4, (i % 4) * C:(i % 4 + 1) * C],
                                           KHf(bi * 2)),
                                       waits=[tok_l, tokAW], inc=sV)
                    ctoks.append(htoks[bi])
                for j in range(8):
                    def t_tr3(e, j=j):
                        for i in range(8):
                            last_i = e.transpose(
                                PTB[:, i * 128:(i + 1) * 128],
                                PSTG[:, i // 4, (i % 4) * C + j * 128:(i % 4) * C + (j + 1) * 128],
                                IDB[:])
                        return last_i
                    tok_tr = p.step("tensor", t_tr3, waits=ctoks[-2:] + [prev_dma], inc=sT)
                    final_tr = tok_tr

                    def v_ev3(e):
                        return e.tensor_copy(STRB[:, 0:512], PTB[:, 0:512])
                    tok_ev = p.step("vector", v_ev3, waits=[tok_tr, prev_dma], inc=sV)
                    half_dma = p.dma("sync", [(wdst[:, j, 0:512], STRB[:, 0:512])], sW,
                                     waits=[tok_ev])
                    tok_ev2 = p.step("vector",
                                     lambda e: e.tensor_copy(STRB[:, 0:512], PTB[:, 512:1024]),
                                     waits=[tok_tr, half_dma], inc=sV)
                    prev_dma = p.dma("sync", [(wdst[:, j, 512:1024], STRB[:, 0:512])], sW,
                                     waits=[tok_ev2])
                    if j % 4 == 3:
                        yield
                tokAW = prev_dma
                if wdst is wrt:
                    last["wkvrt"] = prev_dma
            last["wot"] = prev_dma
            last["pstg_free"] = final_tr
            last["ptb_free"] = tok_ev2
            last["scan"] = prev_dma
            yield "stg_done"

        def emit_in(c):
            nb = c % 3
            tok_e = None
            for tau in range(NTAU):
                tok_d = p.dma("gpsimd", [(TSG[:, :],
                                          x_in[c * TT + tau * 128: c * TT + (tau + 1) * 128, :])],
                              sI, waits=[last["tsg_free"], last["xb"][nb], tok_e])
                for gp in range(2):
                    def t_tin(e, gp=gp):
                        for gg in range(4):
                            g = gp * 4 + gg
                            last_i = e.transpose(PTR[:, gg * 128:(gg + 1) * 128],
                                                 TSG[:, g * 128:(g + 1) * 128], IDF[:])
                        return last_i
                    tok_t = p.step("tensor", t_tin,
                                   waits=[tok_d, last["ptr_free"], tok_e], inc=sT)
                    tok_e = p.step("vector",
                                   lambda e, gp=gp, nb=nb, tau=tau: e.tensor_copy(
                                       XB[nb][:, gp * 4:gp * 4 + 4, tau * 128:(tau + 1) * 128],
                                       PTR[:, 0:512]),
                                   waits=[tok_t], inc=sV)
                    last["ptr_free"] = tok_e
                last["tsg_free"] = tok_e
            last["xb"][nb] = tok_e
            return tok_e

        def emit_ln(c, xv, out_bf, xtok, use_pkvr=False):
            PS = PKVR if use_pkvr else PM1
            freek = "pkvr_free" if use_pkvr else "pm1_free"
            tok_bf = p.step("scalar", lambda e: e.copy(XMF[:], xv[:]),
                            waits=[xtok, last["ln_busy"]], inc=sS)

            def t_sumA(e):
                for g in range(NG):
                    last_i = e.matmul(PS[:, 0:TT], ONESB[:], XMF[:, g, :],
                                      start=(g == 0), stop=(g == NG - 1))
                return last_i
            tok_mmA = p.step("tensor", t_sumA, waits=[tok_bf, last[freek]], inc=sT)
            tok_sq = p.step("scalar", lambda e: e.activation(XMF[:], xv[:], AF.Square),
                            waits=[xtok, tok_mmA], inc=sS)

            def t_sumB(e):
                for g in range(NG):
                    last_i = e.matmul(PS[:, TT:2 * TT], ONESB[:], XMF[:, g, :],
                                      start=(g == 0), stop=(g == NG - 1))
                return last_i
            tok_mm = p.step("tensor", t_sumB, waits=[tok_sq], inc=sT)

            def v_stats(e):
                e.tensor_scalar(MS[:], PS[:, 0:TT], 1.0 / C, None, OP.mult)
                e.tensor_scalar(RS[:], PS[:, TT:2 * TT], 1.0 / C, None, OP.mult)
                e.tensor_tensor(LNB[:, 0, :], MS[:], MS[:], OP.mult)
                return e.tensor_tensor(RS[:], RS[:], LNB[:, 0, :], OP.subtract)
            tok_vs = p.step("vector", v_stats, waits=[tok_mm], inc=sV)
            last[freek] = tok_vs

            def s_stats(e):
                e.activation(RS[:], RS[:], AF.Ln, bias=CEPS[:, 0:1])
                return e.activation(RS[:], RS[:], AF.Exp, scale=-0.5)
            tok_ss = p.step("scalar", s_stats, waits=[tok_vs], inc=sS)

            dst = out_bf if out_bf is not None else LNB

            def v_norm(e):
                e.tensor_tensor(MS[:], MS[:], RS[:], OP.mult)
                e.tensor_tensor(LNB[:], xv[:], bc_row(RS), OP.mult)
                return e.tensor_tensor(dst[:], LNB[:], bc_row(MS), OP.subtract)
            tok_vn = p.step("vector", v_norm, waits=[tok_ss, xtok], inc=sV)
            last["ln_busy"] = tok_vn
            return tok_vn

        def gen_ffn(c, nb, xtok, res, stage=5):
            xv = XB[nb]
            tok_ln = emit_ln(c, xv, None, xtok)
            yield "s1"
            if stage <= 1:
                res["tok"] = tok_ln
                return

            def v_abs(e):
                e.scalar_tensor_tensor(ABSW[:], LNB[:], -1.0, LNB[:], OP.mult, OP.max)
                for g in range(1, NG):
                    last_i = e.tensor_tensor(ABSW[:, 0, :], ABSW[:, 0, :], ABSW[:, g, :],
                                             OP.max)
                return last_i
            tok_va = p.step("vector", v_abs, waits=[tok_ln], inc=sV)

            def t_trx(e):
                for tau in range(NTAU):
                    last_i = e.transpose(PTR[:, tau * 128:(tau + 1) * 128],
                                         ABSW[:, 0, tau * 128:(tau + 1) * 128], IDF[:])
                return last_i
            tok_tt = p.step("tensor", t_trx, waits=[tok_va, last["ptr_free"]], inc=sT)

            def v_xs(e):
                for tau in range(NTAU):
                    vbar(e, e.tensor_reduce(XSC[:, tau:tau + 1],
                                            PTR[:, tau * 128:(tau + 1) * 128], AX, OP.max))
                return e.tensor_scalar_max(XSC[:], XSC[:], EPS)
            tok_vxs = p.step("vector", v_xs, waits=[tok_tt, last["sxs"]], inc=sV)
            last["ptr_free"] = tok_vxs

            def s_xs(e):
                sbar(e, e.activation(LXS[:], XSC[:], AF.Ln))
                e.activation(QCOL[:], LXS[:], AF.Exp, scale=-1.0)
                return e.activation(EXL[:], LXS[:], AF.Exp)
            tok_sxs = p.step("scalar", s_xs,
                             waits=[tok_vxs, last["s1t"], last["sh"]], inc=sS)

            def v_sc(e):
                e.tensor_scalar(QCOL[:], QCOL[:], 127.0, None, OP.mult)
                vbar(e, e.tensor_tensor(ACOL[:], EXL[:],
                                        bass.AP(RSC1, 0, [[1, 128], [0, 2]]), OP.mult))
                return e.tensor_copy(ACOL[:], ACOL[:])
            tok_sxs = p.step("vector", v_sc, waits=[tok_sxs], inc=sV)
            last["sxs"] = tok_sxs
            yield

            def t_q(e):
                e.transpose(PTR[0:1, 0:128], QCOL[:, 0:1], IDF[:])
                return e.transpose(PTR[0:1, 128:256], QCOL[:, 1:2], IDF[:])
            tok_tq = p.step("tensor", t_q, waits=[tok_sxs, last["ptr_free"]], inc=sT)
            tok_vq = p.step("vector", lambda e: e.tensor_copy(QR1[0:1, 0:256], PTR[0:1, 0:256]),
                            waits=[tok_tq], inc=sV)
            last["ptr_free"] = tok_vq

            tok_vir = p.step("vector",
                             lambda e: e.tensor_tensor(IRROW[0:1, 0:256], QR1[0:1, 0:256],
                                                       bass.AP(RS1, 0, [[1, 1], [0, 256]]),
                                                       OP.mult),
                             waits=[tok_vq, last["s1t"]], inc=sV)

            def t_qb(e):
                return e.matmul(PM1[:, 256:512], ONESF[0:1, :], QR1[0:1, 0:256],
                                start=True, stop=True)
            tok_tqb = p.step("tensor", t_qb, waits=[tok_vq, last["pm1_free"]], inc=sT)

            def v_round_in(e):
                e.tensor_tensor(LNB[:], LNB[:], bc_row(PM1, stride=1024, off=256), OP.mult)
                e.tensor_scalar(LNB[:], LNB[:], 128.0, None, OP.add)
                cv = v3(CINT, 2048)
                e.tensor_copy(cv, LNB[:])
                e.scalar_tensor_tensor(ABSW[:], LNB[:], -0.5, cv, OP.add, OP.is_gt)
                return e.scalar_tensor_tensor(XQY[:], cv, -128.0, ABSW[:], OP.add, OP.add)
            tok_xq = p.step("vector", v_round_in, waits=[tok_tqb, tok_va], inc=sV)
            yield "w1"
            if stage <= 2:
                res["tok"] = tok_xq
                return

            half_evac = [tok_xq, tok_xq]
            w1tok = last["w1s"]
            w1dma = None
            gelu_tok = None
            for k in range(16):
                hs, tau = k // 2, k % 2
                wb = hs % 2
                if tau == 0:
                    w1dma = p.dma("sync", [(W1S[wb][:], w1t[:, hs])], sW,
                                  waits=[w1tok[wb], last["w1t"]])

                def t_mm1(e, hs=hs, tau=tau, wb=wb, k=k):
                    ph = PM1[:, (k % 2) * 512:(k % 2 + 1) * 512]
                    for cc in range(NG):
                        e.matmul(ph, XQY[:, cc, tau * 128:(tau + 1) * 128],
                                 W1S[wb][:, cc, :], start=(cc == 0), stop=False)
                    return e.matmul(ph, IRROW[0:1, tau * 128:(tau + 1) * 128],
                                    PSTG[0:1, 2, hs * 512:(hs + 1) * 512],
                                    start=False, stop=True)
                tok_m = p.step("tensor", t_mm1,
                               waits=[w1dma, tok_xq, tok_vir, half_evac[k % 2]] +
                                     ([last["b1r"]] if k == 0 else []), inc=sT)
                if tau == 1:
                    w1tok[wb] = tok_m

                def v_gpre(e, hs=hs, tau=tau, k=k):
                    return e.tensor_tensor(HTM[tau][:, hs * 512:(hs + 1) * 512],
                                           PM1[:, (k % 2) * 512:(k % 2 + 1) * 512],
                                           bass.AP(ACOL, tau, [[2, 128], [0, 512]]),
                                           OP.mult)
                pre_tok = p.step("vector", v_gpre, waits=[tok_m, tok_sxs], inc=sV)
                half_evac[k % 2] = pre_tok

                def s_gelu(e, hs=hs, tau=tau):
                    return e.activation(HTM[tau][:, hs * 512:(hs + 1) * 512],
                                        HTM[tau][:, hs * 512:(hs + 1) * 512], AF.Gelu)
                gelu_tok = p.step("scalar", s_gelu, waits=[pre_tok], inc=sS)
                if k % 4 == 3:
                    yield
            last["pm1_free"] = gelu_tok
            last["w1s"] = w1tok
            yield "s2"
            if stage <= 3:
                res["tok"] = gelu_tok
                return

            def v_habs(e):
                for tau in range(NTAU):
                    vbar(e, e.tensor_reduce(HSC[:, tau:tau + 1], HTM[tau][:], AX, OP.max,
                                            apply_absolute_value=True))
                return e.tensor_scalar_max(HSC[:], HSC[:], EPS)
            tok_vh = p.step("vector", v_habs, waits=[gelu_tok, last["sh"]], inc=sV)

            def s_h(e):
                sbar(e, e.activation(LXS[:], HSC[:], AF.Ln))
                e.activation(QH[:], LXS[:], AF.Exp, scale=-1.0)
                return e.activation(EXL[:], LXS[:], AF.Exp)
            tok_sh = p.step("scalar", s_h, waits=[tok_vh, last["s2t"]], inc=sS)

            def v_sh(e):
                vbar(e, e.tensor_scalar(QH[:], QH[:], 127.0, None, OP.mult))
                vbar(e, e.tensor_tensor(BCOL[:], EXL[:],
                                        bass.AP(RSC2, 0, [[1, 128], [0, 2]]), OP.mult))
                return e.tensor_copy(QH[:], QH[:])
            tok_sh = p.step("vector", v_sh, waits=[tok_sh], inc=sV)
            last["sh"] = tok_sh
            yield

            def t_b(e):
                e.transpose(PTR[0:1, 0:128], BCOL[:, 0:1], IDF[:])
                return e.transpose(PTR[0:1, 128:256], BCOL[:, 1:2], IDF[:])

            def t_bb_fn(e):
                return e.matmul(PM1[:, 256:512], ONESF[0:1, :], QR1[0:1, 0:256],
                                start=True, stop=True)

            tok_tb = p.step("tensor", t_b, waits=[tok_sh, last["ptr_free"]], inc=sT)
            tok_vb = p.step("vector", lambda e: e.tensor_copy(QR1[0:1, 0:256], PTR[0:1, 0:256]),
                            waits=[tok_tb], inc=sV)
            last["ptr_free"] = tok_vb

            tok_tbb = p.step("tensor", t_bb_fn, waits=[tok_vb, last["pm1_free"]], inc=sT)
            tok_vbb = p.step("vector", lambda e: e.tensor_copy(BETA[:], PM1[:, 256:512]),
                             waits=[tok_tbb], inc=sV)
            last["pm1_free"] = tok_vbb
            yield

            yield "stg"
            rtok = tok_sh
            for tau in range(NTAU):
                pw = last["pstg_free"] if tau == 0 else None

                def v_r(e, tau=tau):
                    for hh in range(2):
                        vp = HTM[tau][:, hh * 2048:(hh + 1) * 2048]
                        ci = CINT[:, 0:2048]
                        gb = flat2(LNB, 2048, NG * TT)
                        e.tensor_tensor(vp, vp, bass.AP(QH, tau, [[2, 128], [0, 2048]]),
                                        OP.mult)
                        e.tensor_scalar(vp, vp, 128.0, None, OP.add)
                        e.tensor_copy(ci, vp)
                        e.scalar_tensor_tensor(gb, vp, -0.5, ci, OP.add, OP.is_gt)
                        last_i = e.scalar_tensor_tensor(
                            PSTG[:, tau, hh * 2048:(hh + 1) * 2048], ci, -128.0, gb,
                            OP.add, OP.add)
                    return last_i
                rtok = p.step("vector", v_r, waits=[rtok, pw, gelu_tok], inc=sV)
                yield
            if stage <= 4:
                res["tok"] = rtok
                return

            yield "mm2"
            w2tok = last["w2s"]
            mm2_tok = None
            evh_toks = [None] * NH
            wd_toks = [None] * NH

            def emit_trh(m):
                mi = m % 2

                def t_trh(e, m=m, mi=mi):
                    for tau in range(NTAU):
                        last_i = e.transpose(
                            PTB[:, mi * 512 + tau * 128:mi * 512 + (tau + 1) * 128],
                            PSTG[:, tau, m * 128:(m + 1) * 128], IDB[:])
                    return last_i
                tok_trh = p.step("tensor", t_trh, waits=[rtok, last["ptb_free"]], inc=sT)
                if m == NH - 1:
                    last["pstg_free"] = tok_trh

                def v_evh(e, mi=mi):
                    return e.tensor_copy(XQHF[:, mi, :],
                                         PTB[:, mi * 512:mi * 512 + TT])
                tok_evh = p.step("vector", v_evh, waits=[tok_trh], inc=sV)
                last["ptb_free"] = tok_evh
                evh_toks[m] = tok_evh

            def emit_mm2(m):
                nonlocal mm2_tok
                mi = m % 2
                wb = m % 2

                def t_mm2(e, m=m, mi=mi, wb=wb):
                    for d in range(NG):
                        last_i = e.matmul(PKVR[:, d * TT:(d + 1) * TT],
                                          W2S[wb][:, d * 128:(d + 1) * 128],
                                          XQHF[:, mi, :],
                                          start=(m == 0 and d % 2 == 0),
                                          stop=(m == NH - 1 and d % 2 == 1))
                    return last_i
                mw = [evh_toks[m], wd_toks[m]]
                if m == 0:
                    mw.append(last["pkvr_free"])
                mm2_tok = p.step("tensor", t_mm2, waits=mw, inc=sT)
                w2tok[m % 2] = mm2_tok

            if stage <= 4.5:
                for m in range(NH):
                    emit_trh(m)
                mm2_tok = evh_toks[-1]
            else:
                for m in range(NH):
                    wd_toks[m] = p.dma("sync", [(W2S[m % 2][:], w2t[:, m, :])], sW,
                                       waits=[w2tok[m % 2], last["w2t"]])
                    emit_trh(m)
                    if m >= 1:
                        emit_mm2(m - 1)
                    if m % 8 == 7:
                        yield
                emit_mm2(NH - 1)
            last["w2s"] = w2tok
            if stage <= 4.5:
                res["tok"] = mm2_tok
                return
            if stage <= 4.8:
                tok_dummy = p.step("vector",
                                   lambda e: e.tensor_copy(ABSW[:], v3(PKVR, 2048)),
                                   waits=[mm2_tok], inc=sV)
                last["pkvr_free"] = tok_dummy
                res["tok"] = tok_dummy
                return
            if stage <= 4.9:
                _tb = p.step("tensor", t_b, waits=[tok_sh, last["ptr_free"]], inc=sT)
                _vb = p.step("vector", lambda e: e.tensor_copy(QR1[0:1, 0:256], PTR[0:1, 0:256]),
                             waits=[_tb], inc=sV)
                last["ptr_free"] = _vb
                _tbb = p.step("tensor", t_bb_fn, waits=[_vb, last["pm1_free"]], inc=sT)
                _vbb = p.step("vector", lambda e: e.tensor_copy(BETA[:], PM1[:, 256:512]),
                              waits=[_tbb], inc=sV)
                last["pm1_free"] = _vbb
                tok_dummy = p.step("vector",
                                   lambda e: e.tensor_copy(ABSW[:], v3(PKVR, 2048)),
                                   waits=[mm2_tok, _vbb], inc=sV)
                last["pkvr_free"] = tok_dummy
                res["tok"] = tok_dummy
                return

            def v_epi(e):
                pk = v3(PKVR, 2048)
                e.tensor_tensor(ABSW[:], pk, bc_row(BETA), OP.mult)
                for g in range(NG):
                    last_i = e.scalar_tensor_tensor(xv[:, g, :], ABSW[:, g, :],
                                                    B2H[:, g:g + 1], xv[:, g, :],
                                                    OP.add, OP.add)
                return last_i
            tok_epi = p.step("vector", v_epi, waits=[mm2_tok, tok_vbb, xtok], inc=sV)
            last["pkvr_free"] = tok_epi
            res["tok"] = tok_epi

        def gen_attn(c, nb, xtok, ln2tok_c, ln2tok_n, res):
            xv = XB[nb]
            lv = LN2[c % 3]
            lnx = LN2[(c + 1) % 3]
            TMP1f = KT
            TMP2f = VT

            def v_mix(e):
                e.tensor_tensor(TMP1f[:], lv[:], bc_col(AM), OP.mult)
                e.tensor_tensor(TMP2f[:, :, 1:TT], lv[:, :, 0:TT - 1],
                                bc_col(BM, inner=TT - 1), OP.mult)
                e.tensor_tensor(TMP2f[:, :, 0:1], HAL[:], bc_col(BM, inner=1), OP.mult)
                for g in range(4, NG):
                    e.scalar_tensor_tensor(TMP1f[:, g, 0:TT - 1], lv[:, g, 1:TT],
                                           CM8[:, g:g + 1], TMP1f[:, g, 0:TT - 1],
                                           OP.mult, OP.add)
                    nxt = lnx[:, g, 0:1] if c < NCH - 1 else ZCOL[:, g, :]
                    e.scalar_tensor_tensor(TMP1f[:, g, TT - 1:TT], nxt,
                                           CM8[:, g:g + 1], TMP1f[:, g, TT - 1:TT],
                                           OP.mult, OP.add)
                return e.tensor_tensor(XM[:], TMP1f[:], TMP2f[:], OP.add)
            tok_mix = p.step("vector", v_mix,
                             waits=[ln2tok_c, ln2tok_n, last["scan"]], inc=sV)
            tok_hal = p.step("vector", lambda e: e.tensor_copy(HAL[:], lv[:, :, TT - 1:TT]),
                             waits=[tok_mix], inc=sV)
            yield

            wkstok = last["wks"]
            evac_k = None
            evac_v = None
            prev_mm = None
            for pi, wsrc in enumerate([wkt, wvt, wrt]):
                for j in range(NG):
                    wb = (pi * NG + j) % 3
                    tok_wd = p.dma("sync", [(WKS[wb][:], wsrc[:, j, :])], sW,
                                   waits=[wkstok[wb], last["wkvrt"]])

                    def t_kvr(e, j=j, wb=wb):
                        for i in range(NG):
                            last_i = e.matmul(PKVR[:, i * TT:(i + 1) * TT],
                                              WKS[wb][:, i * 128:(i + 1) * 128],
                                              XM[:, j, :],
                                              start=(j == 0 and i % 2 == 0),
                                              stop=(j == NG - 1 and i % 2 == 1))
                        return last_i
                    mw = [tok_wd, tok_mix, prev_mm]
                    if pi == 0 and j == 0:
                        mw.append(last["pkvr_free"])
                    if pi == 1 and j == 0:
                        mw.append(evac_k)
                    if pi == 2 and j == 0:
                        mw.append(evac_v)
                    prev_mm = p.step("tensor", t_kvr, waits=mw, inc=sT)
                    wkstok[wb] = prev_mm
                if pi == 0:
                    evac_k = p.step("vector",
                                    lambda e: e.tensor_scalar_min(KT[:], v3(PKVR, 2048),
                                                                  KCLAMP),
                                    waits=[prev_mm], inc=sV)
                elif pi == 1:
                    evac_v = p.step("scalar", lambda e: e.copy(VT[:], v3(PKVR, 2048)),
                                    waits=[prev_mm], inc=sS)
                yield
            tok_rmm = prev_mm
            last["wks"] = wkstok

            tok_ke = p.step("scalar", lambda e: e.activation(KT[:], KT[:], AF.Exp),
                            waits=[evac_k], inc=sS)
            yield


            half_y = []
            sg_toks = []
            prev_half = tok_ke
            for hh in range(2):
                g0 = hh * 4
                KTh = KT[:, g0:g0 + 4, :]
                VTh = VT[:, g0:g0 + 4, :]
                rph = bass.AP(PKVR, g0 * TT, [[2048, 128], [TT, 4], [1, TT]])

                def v_mscan(e, g0=g0, KTh=KTh):
                    e.tensor_copy(MT[:, :, 0:1], CARM[:, g0:g0 + 4])
                    for g in range(4):
                        last_i = e.tensor_tensor_scan(MT[:, g, 1:TT + 1], KTh[:, g, :],
                                                      KTh[:, g, :], CARM[:, g0 + g:g0 + g + 1],
                                                      OP.max, OP.max)
                    return last_i
                tok_ms = p.step("vector", v_mscan, waits=[tok_ke, last["scan"], prev_half],
                                inc=sV)

                tok_g1 = p.step("vector",
                                lambda e, KTh=KTh: e.tensor_tensor(GT[:], KTh,
                                                                   MT[:, :, 1:TT + 1],
                                                                   OP.subtract),
                                waits=[tok_ms], inc=sV)
                tok_g2 = p.step("scalar", lambda e: e.activation(GT[:], GT[:], AF.Exp),
                                waits=[tok_g1], inc=sS)
                yield

                def v_lam(e, g0=g0):
                    for g in range(4):
                        last_i = e.scalar_tensor_tensor(TMP1[:, g, :], MT[:, g, 0:TT],
                                                        W8[:, g0 + g:g0 + g + 1],
                                                        MT[:, g, 1:TT + 1],
                                                        OP.add, OP.subtract)
                    return last_i
                tok_l1 = p.step("vector", v_lam, waits=[tok_ms], inc=sV)
                tok_l2 = p.step("scalar", lambda e: e.activation(TMP1[:], TMP1[:], AF.Exp),
                                waits=[tok_l1], inc=sS)
                yield

                def v_scans(e, g0=g0, KTh=KTh, VTh=VTh):
                    e.tensor_tensor(KTh, GT[:], VTh, OP.mult)
                    e.tensor_copy(AT[:, :, 0:1], CARA[:, g0:g0 + 4])
                    e.tensor_copy(BT[:, :, 0:1], CARB[:, g0:g0 + 4])
                    for g in range(4):
                        e.tensor_tensor_scan(AT[:, g, 1:TT + 1], TMP1[:, g, :], KTh[:, g, :],
                                             CARA[:, g0 + g:g0 + g + 1], OP.mult, OP.add)
                    for g in range(4):
                        last_i = e.tensor_tensor_scan(BT[:, g, 1:TT + 1], TMP1[:, g, :],
                                                      GT[:, g, :],
                                                      CARB[:, g0 + g:g0 + g + 1],
                                                      OP.mult, OP.add)
                    return last_i
                tok_sc = p.step("vector", v_scans, waits=[tok_l2, tok_g2], inc=sV)
                yield

                def v_post1(e, g0=g0, VTh=VTh):
                    for g in range(4):
                        e.tensor_scalar(GT[:, g, :], GT[:, g, :],
                                        U8E[:, g0 + g:g0 + g + 1], None, OP.mult)
                    e.tensor_tensor(VTh, GT[:], VTh, OP.mult)
                    for g in range(4):
                        e.tensor_scalar(TMP2[:, g, :], TMP1[:, g, :],
                                        EWN[:, g0 + g:g0 + g + 1], None, OP.mult)
                    e.tensor_tensor(TMP1[:], AT[:, :, 0:TT], TMP2[:], OP.mult)
                    e.tensor_tensor(TMP2[:], BT[:, :, 0:TT], TMP2[:], OP.mult)
                    e.tensor_tensor(TMP2[:], TMP2[:], GT[:], OP.add)
                    return e.tensor_tensor(TMP1[:], TMP1[:], VTh, OP.add)
                tok_p1 = p.step("vector", v_post1, waits=[tok_sc], inc=sV)

                tok_sgh = p.step("scalar",
                                 lambda e, rph=rph: e.activation(GT[:], rph, AF.Exp,
                                                                 scale=-1.0),
                                 waits=[tok_rmm, evac_v, tok_p1], inc=sS)
                tok_sg1 = p.step("vector",
                                 lambda e: e.tensor_scalar(GT[:], GT[:], 1.0, None, OP.add),
                                 waits=[tok_sgh], inc=sV)
                sg_toks.append(tok_sg1)
                tok_dn = p.step("vector",
                                lambda e: e.tensor_tensor(TMP2[:], TMP2[:], GT[:],
                                                          OP.mult),
                                waits=[tok_sg1], inc=sV)
                yield

                def s_rec(e):
                    e.activation(TMP2[:], TMP2[:], AF.Ln)
                    return e.activation(TMP2[:], TMP2[:], AF.Exp, scale=-1.0)
                tok_rc = p.step("scalar", s_rec, waits=[tok_dn], inc=sS)

                def v_y(e, hh=hh, g0=g0):
                    e.tensor_tensor(YB[:, g0:g0 + 4, :], TMP1[:], TMP2[:], OP.mult)
                    e.tensor_copy(CARA[:, g0:g0 + 4], AT[:, :, TT:TT + 1])
                    e.tensor_copy(CARB[:, g0:g0 + 4], BT[:, :, TT:TT + 1])
                    return e.tensor_copy(CARM[:, g0:g0 + 4], MT[:, :, TT:TT + 1])
                tok_y = p.step("vector", v_y, waits=[tok_rc], inc=sV)
                half_y.append(tok_y)
                prev_half = tok_y
                yield
            last["scan"] = half_y[-1]
            tok_y = half_y[-1]

            prev_mm = None
            for j in range(NG):
                wb = j % 3
                tok_wd = p.dma("sync", [(WKS[wb][:], wot[:, j, :])], sW,
                               waits=[wkstok[wb], last["wot"]])

                def t_wo(e, j=j, wb=wb):
                    for i in range(NG):
                        last_i = e.matmul(PKVR[:, i * TT:(i + 1) * TT],
                                          WKS[wb][:, i * 128:(i + 1) * 128],
                                          YB[:, j, :],
                                          start=(j == 0 and i % 2 == 0),
                                          stop=(j == NG - 1 and i % 2 == 1))
                    return last_i
                mw = [tok_wd, half_y[0] if j < 4 else half_y[1], prev_mm]
                if j == 0:
                    mw.extend(sg_toks)
                prev_mm = p.step("tensor", t_wo, waits=mw, inc=sT)
                wkstok[wb] = prev_mm
            yield

            tok_res = p.step("vector",
                             lambda e: e.tensor_tensor(xv[:], xv[:], v3(PKVR, 2048), OP.add),
                             waits=[prev_mm, xtok], inc=sV)
            last["pkvr_free"] = tok_res
            res["tok"] = tok_res

        def emit_out(c, nb, xtok):
            tok_o = None
            for tau in range(NTAU):
                tok_e = None
                for gq in range(2):
                    def t_to(e, tau=tau, gq=gq, nb=nb):
                        for gg in range(4):
                            g = gq * 4 + gg
                            last_i = e.transpose(PTR[:, gg * 128:(gg + 1) * 128],
                                                 XB[nb][:, g, tau * 128:(tau + 1) * 128],
                                                 IDF[:])
                        return last_i
                    tok_t = p.step("tensor", t_to,
                                   waits=[xtok, last["ptr_free"], last["tsg_free"], tok_o],
                                   inc=sT)
                    tok_e = p.step("vector",
                                   lambda e, gq=gq: e.tensor_copy(
                                       TSG[:, gq * 512:(gq + 1) * 512], PTR[:, 0:512]),
                                   waits=[tok_t], inc=sV)
                    last["ptr_free"] = tok_e
                tok_o = p.dma("sync", [(y_out[c * TT + tau * 128: c * TT + (tau + 1) * 128, :],
                                        TSG[:, :])], sO, waits=[tok_e])
                last["tsg_free"] = tok_o
            last["xb"][nb] = tok_o
            return tok_o

        x1tok = [None] * NCH
        ln2tok = [None] * NCH

        _STAGE = {"f_ln": 1, "f_q": 2, "f_mm1": 3, "f_hq": 4, "f_tr": 4.5,
                  "f_mm2": 4.8, "f_bb": 4.9}

        def gen_front(c):
            tok_in = emit_in(c)
            nb = c % 3
            yield "x0"
            if parts in ("full", "ffn1", "ffn1ln", "dbg_ffn1"):
                r = {}
                yield from gen_ffn(c, nb, tok_in, r)
                t1 = r["tok"]
            elif parts in _STAGE:
                r = {}
                yield from gen_ffn(c, nb, tok_in, r, stage=_STAGE[parts])
                t1 = r["tok"]
            else:
                t1 = tok_in
            x1tok[c] = t1
            if parts in ("full", "ffn1ln"):
                ln2tok[c] = emit_ln(c, XB[nb], LN2[c % 3], t1, use_pkvr=True)
            else:
                ln2tok[c] = t1

        def drive(g):
            for _ in g:
                pass

        def interleave(a, b):
            alive_a, alive_b, b_paused = True, True, False
            while alive_a or alive_b:
                if alive_a:
                    try:
                        next(a)
                    except StopIteration:
                        alive_a = False
                        if b_paused:
                            b_paused = False
                if alive_b and not b_paused:
                    try:
                        if next(b) == "mm2" and alive_a:
                            b_paused = True
                    except StopIteration:
                        alive_b = False

        def interleave_hold(a, b):
            alive_b, b_paused = b is not None, False
            while True:
                try:
                    next(a)
                except StopIteration:
                    break
                if alive_b and not b_paused:
                    try:
                        if next(b) == "mm2":
                            b_paused = True
                    except StopIteration:
                        alive_b = False
            return b if alive_b else None

        def gen_tail(c, nb, atok):
            rf = {}
            yield from gen_ffn(c, nb, atok, rf)
            emit_out(c, nb, rf["tok"])

        PRO_MARK = {"s1": "s1_done", "w1": "w1_done", "s2": "s2_done",
                    "stg": "stg_done", "mm2": "stg_done"}

        fronts = {}
        if parts == "full" and PUMP_FRONTS:
            fronts = {ci: gen_front(ci) for ci in range(min(N_PUMP, NCH))}
            flags = set()
            paused = {}
            P = gen_prologue()
            while True:
                try:
                    mk = next(P)
                except StopIteration:
                    break
                if mk:
                    flags.add(mk)
                for ci in fronts:
                    g = fronts[ci]
                    if paused.get(ci) and paused[ci] not in flags:
                        continue
                    paused[ci] = None
                    try:
                        v = next(g)
                    except StopIteration:
                        continue
                    if ci == 0:
                        req = ("stg_done" if v in F0_HOLD else PRO_MARK.get(v)) \
                            if v is not None else None
                    else:
                        req = "stg_done" if v is not None else None
                    if req and req not in flags:
                        paused[ci] = req
            drive(fronts.pop(0))
            if 1 in fronts:
                drive(fronts.pop(1))
            else:
                drive(gen_front(1))
        else:
            g0 = gen_front(0)
            if EARLY_IN0:
                next(g0)
            drive(gen_prologue())
            drive(g0)
            if NCH > 1:
                drive(gen_front(1))

        for c in range(NCH):
            nb = c % 3
            if parts == "full":
                ra = {}
                A = gen_attn(c, nb, x1tok[c], ln2tok[c],
                             ln2tok[c + 1] if c + 1 < NCH else ln2tok[c], ra)
                B = fronts.pop(c + 2, None)
                if B is None and c + 2 < NCH:
                    B = gen_front(c + 2)
                B = interleave_hold(A, B)
                F = gen_tail(c, nb, ra["tok"])
                if B is not None:
                    interleave(B, F)
                else:
                    drive(F)
            else:
                t3 = x1tok[c]
                if c + 2 < NCH:
                    drive(gen_front(c + 2))
                emit_out(c, nb, t3)

        if parts.startswith("dbg"):
            dbg_outs = []
            def dump(name, ap, shape, dt=F32):
                o = nc.dram_tensor("dbg_" + name, shape, dt, kind="ExternalOutput").ap()
                dbg_outs.append((o, ap))
            dump("xqy", XQY[:], [128, NG, TT], BF16)
            dump("htm0", HTM[0][:], [128, H])
            dump("acol", ACOL[:], [128, 2])
            dump("qcol", QCOL[:], [128, 2])
            dump("xsc", XSC[:], [128, 2])
            dump("hsc", HSC[:], [128, 2])
            dump("qh", QH[:], [128, 2])
            dump("rsbeta", RS[:], [128, TT])
            dump("lnb", LNB[:], [128, NG, TT])
            dump("pstg0", PSTG[:, 0, :], [128, H], BF16)
            dump("w1s0", W1S[0][:], [128, NG, 512], BF16)
            dump("ls1", LS1[:], [128, 1])
            dump("rs1", RS1[:], [128, 1])
            dump("irb", IRB[0:1, 0, 0:256], [1, 256], BF16)
            dump("absw", ABSW[:], [128, NG, TT])
            tok_dbg = p.dma("gpsimd", [(o, ap) for (o, ap) in dbg_outs], sI,
                            waits=[last["scan"], last["pkvr_free"], last["xb"][0],
                                   last["xb"][1]])
            p.streams["gpsimd"].append(lambda e: e.wait_ge(sI.h, sI.n))
        final_o = sO.n
        p.streams["sync"].append(lambda e: e.wait_ge(sO.h, final_o))

        with nc.Block() as block:
            block.sync(lambda e: [fn(e) for fn in p.streams["sync"]])
            block.tensor(lambda e: [fn(e) for fn in p.streams["tensor"]])
            block.vector(lambda e: [fn(e) for fn in p.streams["vector"]])
            block.scalar(lambda e: [fn(e) for fn in p.streams["scalar"]])
            block.gpsimd(lambda e: [fn(e) for fn in p.streams["gpsimd"]])

    return nc


def kernel(L0_attn_x, L0_attn_time_decay, L0_attn_time_first, L0_attn_time_mix,
           L0_attn_combined_mix, L0_attn_Wk, L0_attn_Wv, L0_attn_Wr, L0_attn_Wo,
           L0_ffn_W1, L0_ffn_b1, L0_ffn_W2, L0_ffn_b2):
    x = np.ascontiguousarray(np.asarray(L0_attn_x, dtype=np.float32))
    args = {
        "td": np.ascontiguousarray(np.asarray(L0_attn_time_decay, np.float32)),
        "tf": np.ascontiguousarray(np.asarray(L0_attn_time_first, np.float32)),
        "tm": np.ascontiguousarray(np.asarray(L0_attn_time_mix, np.float32)),
        "cm": np.ascontiguousarray(np.asarray(L0_attn_combined_mix, np.float32)),
        "wk": np.ascontiguousarray(np.asarray(L0_attn_Wk, np.float32)),
        "wv": np.ascontiguousarray(np.asarray(L0_attn_Wv, np.float32)),
        "wr": np.ascontiguousarray(np.asarray(L0_attn_Wr, np.float32)),
        "wo": np.ascontiguousarray(np.asarray(L0_attn_Wo, np.float32)),
        "w1": np.ascontiguousarray(np.asarray(L0_ffn_W1, np.float32)),
        "b1": np.ascontiguousarray(np.asarray(L0_ffn_b1, np.float32)),
        "w2": np.ascontiguousarray(np.asarray(L0_ffn_W2, np.float32)),
        "b2": np.ascontiguousarray(np.asarray(L0_ffn_b2, np.float32)),
    }
    nc = build_program(PARTS)
    in_maps = [dict(args, x=np.ascontiguousarray(x[b])) for b in range(8)]
    res = run_bass_kernel_spmd(nc, in_maps, core_ids=list(range(8)))
    return np.stack([np.asarray(r["y"]) for r in res.results], axis=0)


# version 17 (final)
# speedup vs baseline: 1.0892x; 1.0892x over previous
import numpy as np
from contextlib import ExitStack

import concourse.bass as bass
import concourse.mybir as mybir
from concourse.bass_utils import run_bass_kernel_spmd

F32 = mybir.dt.float32
BF16 = mybir.dt.bfloat16
I16 = mybir.dt.int16
OP = mybir.AluOpType
AF = mybir.ActivationFunctionType
AX = mybir.AxisListType.X

T, C, H = 4096, 1024, 4096
NG = C // 128
NH = H // 128
TT = 256
NCH = T // TT
NTAU = TT // 128
QB = 127.0
EPS = 1e-5
KCLAMP = 60.0
LN127 = float(np.log(127.0))

PUMP_FRONTS = True
F0_HOLD = ()
N_PUMP = 1
EARLY_IN0 = False
PARTS = "full"


class _Sem:
    def __init__(self, h):
        self.h = h
        self.n = 0


class _Prog:
    def __init__(self, nc, ctx):
        self.nc = nc
        self.streams = {k: [] for k in ("tensor", "vector", "scalar", "sync", "gpsimd")}
        self.ctx = ctx

    def sem(self, name):
        return _Sem(self.ctx.enter_context(self.nc.semaphore(name)))

    def step(self, eng, fn, waits=(), inc=None, amt=1):
        wl = [(w[0].h, w[1]) for w in waits if w is not None]
        tok = None
        if inc is not None:
            inc.n += amt
            tok = (inc, inc.n)

        def run(e, wl=wl, fn=fn, inc=inc, amt=amt):
            for (h, v) in wl:
                e.wait_ge(h, v)
            last = fn(e)
            if inc is not None:
                assert last is not None, "step with inc must return last instruction"
                last.then_inc(inc.h, amt)
        self.streams[eng].append(run)
        return tok

    def dma(self, eng, pairs, sem, waits=()):
        wl = [(w[0].h, w[1]) for w in waits if w is not None]
        sem.n += 16 * len(pairs)
        tok = (sem, sem.n)

        def run(e, wl=wl, pairs=pairs, sem=sem):
            for (h, v) in wl:
                e.wait_ge(h, v)
            for (dst, src) in pairs:
                e.dma_start(dst, src).then_inc(sem.h, 16)
        self.streams[eng].append(run)
        return tok


def build_program(parts="full"):
    nc = bass.Bass("TRN2", target_bir_lowering=False, detect_race_conditions=False)

    x_in = nc.dram_tensor("x", [T, C], F32, kind="ExternalInput").ap()
    td_in = nc.dram_tensor("td", [C], F32, kind="ExternalInput").ap()
    tf_in = nc.dram_tensor("tf", [C], F32, kind="ExternalInput").ap()
    tm_in = nc.dram_tensor("tm", [C], F32, kind="ExternalInput").ap()
    cm_in = nc.dram_tensor("cm", [C], F32, kind="ExternalInput").ap()
    wk_in = nc.dram_tensor("wk", [C, C], F32, kind="ExternalInput").ap()
    wv_in = nc.dram_tensor("wv", [C, C], F32, kind="ExternalInput").ap()
    wr_in = nc.dram_tensor("wr", [C, C], F32, kind="ExternalInput").ap()
    wo_in = nc.dram_tensor("wo", [C, C], F32, kind="ExternalInput").ap()
    w1_in = nc.dram_tensor("w1", [H, C], F32, kind="ExternalInput").ap()
    b1_in = nc.dram_tensor("b1", [H], F32, kind="ExternalInput").ap()
    w2_in = nc.dram_tensor("w2", [C, H], F32, kind="ExternalInput").ap()
    b2_in = nc.dram_tensor("b2", [C], F32, kind="ExternalInput").ap()
    y_out = nc.dram_tensor("y", [T, C], F32, kind="ExternalOutput").ap()

    w1t = nc.dram_tensor("w1t", [128, 8, 8, 512], BF16, kind="Internal").ap()
    w2t = nc.dram_tensor("w2t", [128, NH, C], BF16, kind="Internal").ap()
    wkt = nc.dram_tensor("wkt", [128, NG, C], BF16, kind="Internal").ap()
    wvt = nc.dram_tensor("wvt", [128, NG, C], BF16, kind="Internal").ap()
    wrt = nc.dram_tensor("wrt", [128, NG, C], BF16, kind="Internal").ap()
    wot = nc.dram_tensor("wot", [128, NG, C], BF16, kind="Internal").ap()

    with ExitStack() as ctx:
        p = _Prog(nc, ctx)
        sb = lambda name, shape, dt=F32: ctx.enter_context(nc.sbuf_tensor(name, shape, dt))
        ps_ = lambda name, shape, dt=F32: ctx.enter_context(nc.psum_tensor(name, shape, dt))

        W8 = sb("W8", [128, NG]); EWN = sb("EWN", [128, NG]); U8E = sb("U8E", [128, NG])
        AM = sb("AM", [128, NG]); BM = sb("BM", [128, NG]); CM8 = sb("CM8", [128, NG])
        B2H = sb("B2H", [128, NG])
        VG = sb("VG", [32, 128])
        CEPS = sb("CEPS", [128, 1]); CL127 = sb("CL127", [128, 1]); C128 = sb("C128", [128, 1])
        LS1 = sb("LS1", [128, 1]); LB2S = sb("LB2S", [128, 1])
        RS1 = sb("RS1", [128, 1]); RS2 = sb("RS2", [128, 1])
        TH1 = sb("TH1", [128, 1]); TH2 = sb("TH2", [128, 1])
        RSC1 = sb("RSC1", [128, 1]); RSC2 = sb("RSC2", [128, 1])
        EXL = sb("EXL", [128, 2])
        SACC = sb("SACC", [128, 1]); SRED = sb("SRED", [128, 1])
        ONESF = sb("ONESF", [128, 128]); ONESB = sb("ONESB", [128, 128], BF16)
        IDF = sb("IDF", [128, 128]); IDB = sb("IDB", [128, 128], BF16)
        B1F = sb("B1F", [128, NH])

        XB = [sb(f"XB{i}", [128, NG, TT]) for i in range(3)]
        TSG = sb("TSG", [128, C])
        LNB = sb("LNB", [128, NG, TT])
        LN2 = [sb(f"LN2_{i}", [128, NG, TT], BF16) for i in range(3)]
        HAL = sb("HAL", [128, NG, 1]); ZCOL = sb("ZCOL", [128, NG, 1])
        XM = sb("XM", [128, NG, TT], BF16)
        XMF = sb("XMF", [128, NG, TT], BF16)
        XQY = sb("XQY", [128, NG, TT], BF16)
        YB = sb("YB", [128, NG, TT], BF16)
        ABSW = sb("ABSW", [128, NG, TT])
        CINT = sb("CINT", [128, 2048], I16)
        KT = sb("KT", [128, NG, TT]); VT = sb("VT", [128, NG, TT])
        GT = sb("GT", [128, 4, TT])
        TMP1 = sb("TMP1", [128, 4, TT]); TMP2 = sb("TMP2", [128, 4, TT])
        MT = sb("MT", [128, 4, TT + 1]); AT = sb("AT", [128, 4, TT + 1])
        BT = sb("BT", [128, 4, TT + 1])
        CARM = sb("CARM", [128, NG]); CARA = sb("CARA", [128, NG]); CARB = sb("CARB", [128, NG])
        HTM = [sb(f"HTM{i}", [128, H]) for i in range(2)]
        PSTG = sb("PSTG", [128, 3, H], BF16)
        XQHF = sb("XQHF", [128, 2, TT], BF16)
        STRB = sb("STRB", [128, 512], BF16)
        W1S = [sb(f"W1S{i}", [128, NG, 512], BF16) for i in range(2)]
        W2S = [sb(f"W2S{i}", [128, C], BF16) for i in range(2)]
        WKS = [sb(f"WKS{i}", [128, C], BF16) for i in range(3)]
        MS = sb("MS", [128, TT]); RS = sb("RS", [128, TT])
        XSC = sb("XSC", [128, 2]); LXS = sb("LXS", [128, 2])
        QCOL = sb("QCOL", [128, 2]); ACOL = sb("ACOL", [128, 2])
        HSC = sb("HSC", [128, 2]); QH = sb("QH", [128, 2]); BCOL = sb("BCOL", [128, 2])
        QR1 = sb("QR1", [1, TT])
        IRROW = sb("IRROW", [1, TT], BF16)
        BETA = sb("BETA", [128, TT])

        PM1 = ps_("PM1", [128, 1024])
        PKVR = ps_("PKVR", [128, 2048])
        PTR = ps_("PTR", [128, 512])
        PTB = ps_("PTB", [128, 1024], BF16)

        sT = p.sem("sT"); sV = p.sem("sV"); sS = p.sem("sS"); sG = p.sem("sG")
        sW = p.sem("sW"); sI = p.sem("sI"); sO = p.sem("sO")
        sBV = p.sem("sBV"); sBS = p.sem("sBS")

        def vbar(e, inst):
            sBV.n += 1
            inst.then_inc(sBV.h, 1)
            e.wait_ge(sBV.h, sBV.n)

        def sbar(e, inst):
            sBS.n += 1
            inst.then_inc(sBS.h, 1)
            e.wait_ge(sBS.h, sBS.n)

        def v3(h, free_total, n=NG, inner=TT, off=0):
            return bass.AP(h, off, [[free_total, 128], [inner, n], [1, inner]])

        def bc_row(h, inner=TT, n=NG, stride=None, off=0):
            return bass.AP(h, off, [[stride if stride is not None else inner, 128],
                                    [0, n], [1, inner]])

        def bc_col(h, n=NG, inner=TT):
            return bass.AP(h, 0, [[n, 128], [1, n], [0, inner]])

        def flat2(h, ncols, free_total, off=0):
            return bass.AP(h, off, [[free_total, 128], [1, ncols]])

        def g_init(e):
            e.memset(ONESF[:], 1.0)
            e.memset(ONESB[:], 1.0)
            e.memset(IDF[:], 0.0)
            e.affine_select(out=IDF[:], in_=IDF[:], compare_op=OP.not_equal,
                            fill=1.0, base=0, pattern=[[-1, 128]], channel_multiplier=1)
            e.memset(IDB[:], 0.0)
            e.affine_select(out=IDB[:], in_=IDB[:], compare_op=OP.not_equal,
                            fill=1.0, base=0, pattern=[[-1, 128]], channel_multiplier=1)
            e.memset(CEPS[:], EPS)
            e.memset(CL127[:], LN127)
            e.memset(C128[:], 128.0)
            e.memset(ZCOL[:], 0.0)
            e.memset(HAL[:], 0.0)
            e.memset(CARA[:], 0.0)
            e.memset(CARB[:], 0.0)
            return e.memset(CARM[:], -1e38)
        tok_ginit = p.step("gpsimd", g_init, inc=sG)

        tok_b1r = p.dma("gpsimd", [(PSTG[0:1, 2, 0:H], b1_in[None, :])], sI,
                        waits=[tok_ginit])

        vecs = [(td_in, W8), (tf_in, U8E), (tm_in, AM), (cm_in, CM8), (b2_in, B2H)]
        prev_tok = tok_ginit
        for (vin, vdst) in vecs:
            tok_l = p.dma("sync", [(VG[0:8, :], vin.rearrange("(g p) -> g p", p=128))],
                          sW, waits=[prev_tok])
            tok_t = p.step("tensor", lambda e: e.transpose(PTR[:, 0:8], VG[0:8, :], IDF[0:8, 0:8]),
                           waits=[tok_l, tok_ginit, prev_tok], inc=sT)
            prev_tok = p.step("vector", lambda e, vd=vdst: e.tensor_copy(vd[:], PTR[:, 0:8]),
                              waits=[tok_t], inc=sV)
        tok_params = prev_tok

        tok_b1l = p.dma("sync", [(VG[0:32, :], b1_in.rearrange("(g p) -> g p", p=128))],
                        sW, waits=[tok_params])
        tok_b1t = p.step("tensor", lambda e: e.transpose(PTR[:, 0:32], VG[0:32, :], IDF[0:32, 0:32]),
                         waits=[tok_b1l], inc=sT)
        tok_b1 = p.step("vector", lambda e: e.tensor_copy(B1F[:], PTR[:, 0:32]),
                        waits=[tok_b1t], inc=sV)

        def v_derived(e):
            e.tensor_scalar(W8[:], W8[:], 1.0 / T, None, OP.mult)
            vbar(e, e.tensor_scalar(BM[:], AM[:], -1.0, 1.0, OP.mult, OP.add))
            e.tensor_tensor(BM[:, 0:4], BM[:, 0:4], CM8[:, 0:4], OP.add)
            return e.tensor_scalar(B2H[:], B2H[:], 0.5, None, OP.mult)
        tok_vd = p.step("vector", v_derived, waits=[tok_params, tok_b1], inc=sV)

        def s_derived(e):
            e.activation(EWN[:], W8[:], AF.Exp, scale=-1.0)
            return e.activation(U8E[:], U8E[:], AF.Exp, scale=1.0 / T)
        tok_sd = p.step("scalar", s_derived, waits=[tok_vd], inc=sS)
        tok_base = tok_sd

        def q_round(e, vp, ci, gb):
            e.tensor_copy(ci, vp)
            e.scalar_tensor_tensor(gb, vp, -0.5, ci, OP.add, OP.is_gt)
            return e.scalar_tensor_tensor(vp, ci, -128.0, gb, OP.add, OP.add)

        last = {
            "pm1_free": tok_base, "pkvr_free": tok_base,
            "ptr_free": tok_b1, "ptb_free": tok_base,
            "xb": [tok_base, tok_base, tok_base], "tsg_free": tok_base,
            "w1s": [tok_base, tok_base], "w2s": [tok_base, tok_base],
            "wks": [tok_base, tok_base, tok_base], "scan": tok_base,
            "sxs": None, "sh": None, "ln_busy": tok_base, "pstg_free": tok_base,
            "s1t": tok_base, "s2t": tok_base, "b1r": tok_b1r,
            "w1t": tok_base, "w2t": tok_base, "wkvrt": tok_base, "wot": tok_base,
        }

        def KHf(q):
            return flat2(KT if q < 2 else VT, C, NG * TT, off=(q % 2) * C)

        def KH2f(b):
            return flat2(KT if b == 0 else VT, 2 * C, NG * TT)

        def gen_prologue():
            tok_zacc = p.step("vector", lambda e: e.memset(SACC[:], 0.0), inc=sV)
            rtoks = [tok_zacc] * 4
            for t in range(32):
                q = t % 4
                tok_l = p.dma("sync", [(KHf(q), w1_in[t * 128:(t + 1) * 128, :])],
                              sW, waits=[rtoks[q]])

                def v_abs(e, q=q):
                    vbar(e, e.tensor_reduce(SRED[:, 0:1], KHf(q), AX,
                                            OP.add, apply_absolute_value=True))
                    vbar(e, e.tensor_tensor(SACC[:], SACC[:], SRED[:], OP.add))
                    return e.tensor_copy(SACC[:], SACC[:])
                rtoks[q] = p.step("vector", v_abs, waits=[tok_l], inc=sV)
                if t % 8 == 7:
                    yield
            tok_s1mm = p.step("tensor", lambda e: e.matmul(PKVR[:, 0:1], ONESF[:],
                                                           SACC[:, 0:1],
                                                           start=True, stop=True),
                              waits=[rtoks[3], last["pkvr_free"]], inc=sT)

            def v_s1(e):
                vbar(e, e.tensor_scalar(SRED[:], PKVR[:, 0:1], 1.0 / (H * C), None, OP.mult))
                vbar(e, e.tensor_scalar_max(SRED[:], SRED[:], EPS))
                e.tensor_scalar(RSC1[:], SRED[:], 1.0 / 127.0, None, OP.mult)
                return e.tensor_scalar(TH1[:], SRED[:], 0.5, None, OP.mult)
            tok_vs1 = p.step("vector", v_s1, waits=[tok_s1mm], inc=sV)
            last["pkvr_free"] = tok_vs1

            def s_s1(e):
                e.activation(LS1[:], SRED[:], AF.Ln, scale=1.0 / 127.0)
                sbar(e, e.activation(RS1[:], SRED[:], AF.Ln))
                return e.activation(RS1[:], RS1[:], AF.Exp, scale=-1.0)
            tok_ss1 = p.step("scalar", s_s1, waits=[tok_vs1], inc=sS)
            last["s1t"] = tok_ss1
            yield "s1_done"

            tB_done = tok_ss1
            prev_dma = None
            for hs in range(8):
                tok_l = p.dma("sync", [(KHf(i), w1_in[(hs * 4 + i) * 128:
                                                      (hs * 4 + i + 1) * 128, :])
                                       for i in range(4)], sW, waits=[tB_done])

                def v_q(e):
                    thb = bass.AP(TH1, 0, [[1, 128], [0, C]])
                    for i in range(4):
                        vp = KHf(i)
                        gb = flat2(GT, C, 4 * TT)
                        e.scalar_tensor_tensor(gb, vp, 1.0, thb, OP.mult, OP.is_gt)
                        e.scalar_tensor_tensor(vp, vp, -1.0, thb, OP.mult, OP.is_gt)
                        last_i = e.tensor_tensor(PSTG[:, 0, i * C:(i + 1) * C],
                                                 gb, vp, OP.subtract)
                    return last_i
                tok_vq = p.step("vector", v_q, waits=[tok_l, tok_ss1], inc=sV)

                for j in range(8):
                    def t_tr(e, j=j):
                        for i in range(4):
                            last_i = e.transpose(PTB[:, i * 128:(i + 1) * 128],
                                                 PSTG[:, 0, i * C + j * 128: i * C + (j + 1) * 128],
                                                 IDB[:])
                        return last_i
                    tok_tr = p.step("tensor", t_tr, waits=[tok_vq, tok_ginit, prev_dma], inc=sT)
                    tok_ev = p.step("vector", lambda e: e.tensor_copy(STRB[:, 0:512], PTB[:, 0:512]),
                                    waits=[tok_tr, prev_dma], inc=sV)
                    prev_dma = p.dma("sync", [(w1t[:, hs, j, :], STRB[:, 0:512])], sW,
                                     waits=[tok_ev])
                tB_done = prev_dma
                yield
            last["w1t"] = prev_dma
            yield "w1_done"

            tok_zacc2 = p.step("vector", lambda e: e.memset(SACC[:], 0.0),
                               waits=[tB_done, tok_ss1, tok_s1mm], inc=sV)
            rtoks2 = [tok_zacc2, tok_zacc2]
            for it in range(16):
                bi = it % 2
                tok_l = p.dma("sync", [(KH2f(bi), w2_in.rearrange("d (a h) -> d a h", a=2)
                                        [it // 2 * 128:(it // 2 + 1) * 128, it % 2, :])],
                              sW, waits=[rtoks2[bi]])

                def v_abs2(e, bi=bi):
                    vbar(e, e.tensor_reduce(SRED[:, 0:1], KH2f(bi), AX, OP.add,
                                            apply_absolute_value=True))
                    vbar(e, e.tensor_tensor(SACC[:], SACC[:], SRED[:], OP.add))
                    return e.tensor_copy(SACC[:], SACC[:])
                rtoks2[bi] = p.step("vector", v_abs2, waits=[tok_l], inc=sV)
                if it % 4 == 3:
                    yield
            tok_s2mm = p.step("tensor", lambda e: e.matmul(PKVR[:, 0:1], ONESF[:],
                                                           SACC[:, 0:1],
                                                           start=True, stop=True),
                              waits=[rtoks2[0], rtoks2[1], last["pkvr_free"]], inc=sT)

            def v_s2(e):
                vbar(e, e.tensor_scalar(SRED[:], PKVR[:, 0:1], 1.0 / (H * C), None, OP.mult))
                vbar(e, e.tensor_scalar_max(SRED[:], SRED[:], EPS))
                e.tensor_scalar(RSC2[:], SRED[:], 0.5 / 127.0, None, OP.mult)
                return e.tensor_scalar(TH2[:], SRED[:], 0.5, None, OP.mult)
            tok_vs2 = p.step("vector", v_s2, waits=[tok_s2mm], inc=sV)
            last["pkvr_free"] = tok_vs2

            def s_s2(e):
                e.activation(LB2S[:], SRED[:], AF.Ln, scale=0.5 / 127.0)
                sbar(e, e.activation(RS2[:], SRED[:], AF.Ln))
                return e.activation(RS2[:], RS2[:], AF.Exp, scale=-1.0)
            tok_ss2 = p.step("scalar", s_s2, waits=[tok_vs2], inc=sS)
            last["s2t"] = tok_ss2
            yield "s2_done"

            tB2 = tok_ss2
            for quar in range(4):
                for i in range(2):
                    dti = quar * 2 + i
                    tok_l = p.dma("sync", [(KH2f(a), w2_in.rearrange("d (a h) -> d a h", a=2)
                                            [dti * 128:(dti + 1) * 128, a, :])
                                           for a in range(2)],
                                  sW, waits=[tB2])

                    def v_q2(e, i=i):
                        thb = bass.AP(TH2, 0, [[1, 128], [0, C]])
                        for q in range(4):
                            vp = KHf(q)
                            gb = flat2(GT, C, 4 * TT)
                            e.scalar_tensor_tensor(gb, vp, 1.0, thb, OP.mult, OP.is_gt)
                            e.scalar_tensor_tensor(vp, vp, -1.0, thb, OP.mult, OP.is_gt)
                            last_i = e.tensor_tensor(PSTG[:, i, q * C:(q + 1) * C],
                                                     gb, vp, OP.subtract)
                        return last_i
                    tB2 = p.step("vector", v_q2, waits=[tok_l, tok_ss2], inc=sV)
                for m in range(NH):
                    def t_tr2(e, m=m):
                        for i in range(2):
                            last_i = e.transpose(PTB[:, i * 128:(i + 1) * 128],
                                                 PSTG[:, i, m * 128:(m + 1) * 128], IDB[:])
                        return last_i
                    tok_tr = p.step("tensor", t_tr2, waits=[tB2, prev_dma], inc=sT)
                    tok_ev = p.step("vector", lambda e: e.tensor_copy(STRB[:, 0:256], PTB[:, 0:256]),
                                    waits=[tok_tr, prev_dma], inc=sV)
                    prev_dma = p.dma("sync", [(w2t[:, m, quar * 256:(quar + 1) * 256],
                                               STRB[:, 0:256])], sW, waits=[tok_ev])
                    if m % 8 == 7:
                        yield
                tB2 = prev_dma
            last["w2t"] = prev_dma

            tokAW = tB2
            final_tr = None
            for (win, wdst) in [(wk_in, wkt), (wv_in, wvt), (wr_in, wrt), (wo_in, wot)]:
                htoks = [tokAW, tokAW]
                ctoks = []
                for i in range(8):
                    bi = i % 2
                    tok_l = p.dma("sync", [(KHf(bi * 2), win[i * 128:(i + 1) * 128, :])], sW,
                                  waits=[htoks[bi]])
                    htoks[bi] = p.step("vector",
                                       lambda e, i=i, bi=bi: e.tensor_copy(
                                           PSTG[:, i // 4, (i % 4) * C:(i % 4 + 1) * C],
                                           KHf(bi * 2)),
                                       waits=[tok_l, tokAW], inc=sV)
                    ctoks.append(htoks[bi])
                for j in range(8):
                    def t_tr3(e, j=j):
                        for i in range(8):
                            last_i = e.transpose(
                                PTB[:, i * 128:(i + 1) * 128],
                                PSTG[:, i // 4, (i % 4) * C + j * 128:(i % 4) * C + (j + 1) * 128],
                                IDB[:])
                        return last_i
                    tok_tr = p.step("tensor", t_tr3, waits=ctoks[-2:] + [prev_dma], inc=sT)
                    final_tr = tok_tr

                    def v_ev3(e):
                        return e.tensor_copy(STRB[:, 0:512], PTB[:, 0:512])
                    tok_ev = p.step("vector", v_ev3, waits=[tok_tr, prev_dma], inc=sV)
                    half_dma = p.dma("sync", [(wdst[:, j, 0:512], STRB[:, 0:512])], sW,
                                     waits=[tok_ev])
                    tok_ev2 = p.step("vector",
                                     lambda e: e.tensor_copy(STRB[:, 0:512], PTB[:, 512:1024]),
                                     waits=[tok_tr, half_dma], inc=sV)
                    prev_dma = p.dma("sync", [(wdst[:, j, 512:1024], STRB[:, 0:512])], sW,
                                     waits=[tok_ev2])
                    if j % 4 == 3:
                        yield
                tokAW = prev_dma
                if wdst is wrt:
                    last["wkvrt"] = prev_dma
            last["wot"] = prev_dma
            last["pstg_free"] = final_tr
            last["ptb_free"] = tok_ev2
            last["scan"] = prev_dma
            yield "stg_done"

        def emit_in(c):
            nb = c % 3
            tok_e = None
            for tau in range(NTAU):
                tok_d = p.dma("gpsimd", [(TSG[:, :],
                                          x_in[c * TT + tau * 128: c * TT + (tau + 1) * 128, :])],
                              sI, waits=[last["tsg_free"], last["xb"][nb], tok_e])
                for gp in range(2):
                    def t_tin(e, gp=gp):
                        for gg in range(4):
                            g = gp * 4 + gg
                            last_i = e.transpose(PTR[:, gg * 128:(gg + 1) * 128],
                                                 TSG[:, g * 128:(g + 1) * 128], IDF[:])
                        return last_i
                    tok_t = p.step("tensor", t_tin,
                                   waits=[tok_d, last["ptr_free"], tok_e], inc=sT)
                    tok_e = p.step("vector",
                                   lambda e, gp=gp, nb=nb, tau=tau: e.tensor_copy(
                                       XB[nb][:, gp * 4:gp * 4 + 4, tau * 128:(tau + 1) * 128],
                                       PTR[:, 0:512]),
                                   waits=[tok_t], inc=sV)
                    last["ptr_free"] = tok_e
                last["tsg_free"] = tok_e
            last["xb"][nb] = tok_e
            return tok_e

        def emit_ln(c, xv, out_bf, xtok, use_pkvr=False):
            PS = PKVR if use_pkvr else PM1
            freek = "pkvr_free" if use_pkvr else "pm1_free"
            tok_bf = p.step("scalar", lambda e: e.copy(XMF[:], xv[:]),
                            waits=[xtok, last["ln_busy"]], inc=sS)

            def t_sumA(e):
                for g in range(NG):
                    last_i = e.matmul(PS[:, 0:TT], ONESB[:], XMF[:, g, :],
                                      start=(g == 0), stop=(g == NG - 1))
                return last_i
            tok_mmA = p.step("tensor", t_sumA, waits=[tok_bf, last[freek]], inc=sT)
            tok_sq = p.step("scalar", lambda e: e.activation(XMF[:], xv[:], AF.Square),
                            waits=[xtok, tok_mmA], inc=sS)

            def t_sumB(e):
                for g in range(NG):
                    last_i = e.matmul(PS[:, TT:2 * TT], ONESB[:], XMF[:, g, :],
                                      start=(g == 0), stop=(g == NG - 1))
                return last_i
            tok_mm = p.step("tensor", t_sumB, waits=[tok_sq], inc=sT)

            def v_stats(e):
                e.tensor_scalar(MS[:], PS[:, 0:TT], 1.0 / C, None, OP.mult)
                e.tensor_scalar(RS[:], PS[:, TT:2 * TT], 1.0 / C, None, OP.mult)
                e.tensor_tensor(LNB[:, 0, :], MS[:], MS[:], OP.mult)
                return e.tensor_tensor(RS[:], RS[:], LNB[:, 0, :], OP.subtract)
            tok_vs = p.step("vector", v_stats, waits=[tok_mm], inc=sV)
            last[freek] = tok_vs

            def s_stats(e):
                e.activation(RS[:], RS[:], AF.Ln, bias=CEPS[:, 0:1])
                return e.activation(RS[:], RS[:], AF.Exp, scale=-0.5)
            tok_ss = p.step("scalar", s_stats, waits=[tok_vs], inc=sS)

            dst = out_bf if out_bf is not None else LNB

            def v_norm(e):
                e.tensor_tensor(MS[:], MS[:], RS[:], OP.mult)
                e.tensor_tensor(LNB[:], xv[:], bc_row(RS), OP.mult)
                return e.tensor_tensor(dst[:], LNB[:], bc_row(MS), OP.subtract)
            tok_vn = p.step("vector", v_norm, waits=[tok_ss, xtok], inc=sV)
            last["ln_busy"] = tok_vn
            return tok_vn

        def gen_ffn(c, nb, xtok, res, stage=5):
            xv = XB[nb]
            tok_ln = emit_ln(c, xv, None, xtok)
            yield "s1"
            if stage <= 1:
                res["tok"] = tok_ln
                return

            def v_abs(e):
                e.scalar_tensor_tensor(ABSW[:], LNB[:], -1.0, LNB[:], OP.mult, OP.max)
                for g in range(1, NG):
                    last_i = e.tensor_tensor(ABSW[:, 0, :], ABSW[:, 0, :], ABSW[:, g, :],
                                             OP.max)
                return last_i
            tok_va = p.step("vector", v_abs, waits=[tok_ln], inc=sV)

            def t_trx(e):
                for tau in range(NTAU):
                    last_i = e.transpose(PTR[:, tau * 128:(tau + 1) * 128],
                                         ABSW[:, 0, tau * 128:(tau + 1) * 128], IDF[:])
                return last_i
            tok_tt = p.step("tensor", t_trx, waits=[tok_va, last["ptr_free"]], inc=sT)

            def v_xs(e):
                for tau in range(NTAU):
                    vbar(e, e.tensor_reduce(XSC[:, tau:tau + 1],
                                            PTR[:, tau * 128:(tau + 1) * 128], AX, OP.max))
                return e.tensor_scalar_max(XSC[:], XSC[:], EPS)
            tok_vxs = p.step("vector", v_xs, waits=[tok_tt, last["sxs"]], inc=sV)
            last["ptr_free"] = tok_vxs

            def s_xs(e):
                sbar(e, e.activation(LXS[:], XSC[:], AF.Ln))
                e.activation(QCOL[:], LXS[:], AF.Exp, scale=-1.0)
                return e.activation(EXL[:], LXS[:], AF.Exp)
            tok_sxs = p.step("scalar", s_xs,
                             waits=[tok_vxs, last["s1t"], last["sh"]], inc=sS)

            def v_sc(e):
                e.tensor_scalar(QCOL[:], QCOL[:], 127.0, None, OP.mult)
                vbar(e, e.tensor_tensor(ACOL[:], EXL[:],
                                        bass.AP(RSC1, 0, [[1, 128], [0, 2]]), OP.mult))
                return e.tensor_copy(ACOL[:], ACOL[:])
            tok_sxs = p.step("vector", v_sc, waits=[tok_sxs], inc=sV)
            last["sxs"] = tok_sxs
            yield

            def t_q(e):
                e.transpose(PTR[0:1, 0:128], QCOL[:, 0:1], IDF[:])
                return e.transpose(PTR[0:1, 128:256], QCOL[:, 1:2], IDF[:])
            tok_tq = p.step("tensor", t_q, waits=[tok_sxs, last["ptr_free"]], inc=sT)
            tok_vq = p.step("vector", lambda e: e.tensor_copy(QR1[0:1, 0:256], PTR[0:1, 0:256]),
                            waits=[tok_tq], inc=sV)
            last["ptr_free"] = tok_vq

            tok_vir = p.step("vector",
                             lambda e: e.tensor_tensor(IRROW[0:1, 0:256], QR1[0:1, 0:256],
                                                       bass.AP(RS1, 0, [[1, 1], [0, 256]]),
                                                       OP.mult),
                             waits=[tok_vq, last["s1t"]], inc=sV)

            def t_qb(e):
                return e.matmul(PM1[:, 256:512], ONESF[0:1, :], QR1[0:1, 0:256],
                                start=True, stop=True)
            tok_tqb = p.step("tensor", t_qb, waits=[tok_vq, last["pm1_free"]], inc=sT)

            def v_round_in(e):
                e.tensor_tensor(LNB[:], LNB[:], bc_row(PM1, stride=1024, off=256), OP.mult)
                e.tensor_scalar(LNB[:], LNB[:], 128.0, None, OP.add)
                cv = v3(CINT, 2048)
                e.tensor_copy(cv, LNB[:])
                e.scalar_tensor_tensor(ABSW[:], LNB[:], -0.5, cv, OP.add, OP.is_gt)
                return e.scalar_tensor_tensor(XQY[:], cv, -128.0, ABSW[:], OP.add, OP.add)
            tok_xq = p.step("vector", v_round_in, waits=[tok_tqb, tok_va], inc=sV)
            yield "w1"
            if stage <= 2:
                res["tok"] = tok_xq
                return

            half_evac = [tok_xq, tok_xq]
            w1tok = last["w1s"]
            w1dma = None
            gelu_tok = None
            for k in range(16):
                hs, tau = k // 2, k % 2
                wb = hs % 2
                if tau == 0:
                    w1dma = p.dma("sync", [(W1S[wb][:], w1t[:, hs])], sW,
                                  waits=[w1tok[wb], last["w1t"]])

                def t_mm1(e, hs=hs, tau=tau, wb=wb, k=k):
                    ph = PM1[:, (k % 2) * 512:(k % 2 + 1) * 512]
                    for cc in range(NG):
                        e.matmul(ph, XQY[:, cc, tau * 128:(tau + 1) * 128],
                                 W1S[wb][:, cc, :], start=(cc == 0), stop=False)
                    return e.matmul(ph, IRROW[0:1, tau * 128:(tau + 1) * 128],
                                    PSTG[0:1, 2, hs * 512:(hs + 1) * 512],
                                    start=False, stop=True)
                tok_m = p.step("tensor", t_mm1,
                               waits=[w1dma, tok_xq, tok_vir, half_evac[k % 2]] +
                                     ([last["b1r"]] if k == 0 else []), inc=sT)
                if tau == 1:
                    w1tok[wb] = tok_m

                def s_gelu(e, hs=hs, tau=tau, k=k):
                    return e.activation(HTM[tau][:, hs * 512:(hs + 1) * 512],
                                        PM1[:, (k % 2) * 512:(k % 2 + 1) * 512],
                                        AF.Gelu, scale=ACOL[:, tau:tau + 1])
                gelu_tok = p.step("scalar", s_gelu, waits=[tok_m, tok_sxs], inc=sS)
                half_evac[k % 2] = gelu_tok
                if k % 4 == 3:
                    yield
            last["pm1_free"] = gelu_tok
            last["w1s"] = w1tok
            yield "s2"
            if stage <= 3:
                res["tok"] = gelu_tok
                return

            def v_habs(e):
                for tau in range(NTAU):
                    vbar(e, e.tensor_reduce(HSC[:, tau:tau + 1], HTM[tau][:], AX, OP.max,
                                            apply_absolute_value=True))
                return e.tensor_scalar_max(HSC[:], HSC[:], EPS)
            tok_vh = p.step("vector", v_habs, waits=[gelu_tok, last["sh"]], inc=sV)

            def s_h(e):
                sbar(e, e.activation(LXS[:], HSC[:], AF.Ln))
                e.activation(QH[:], LXS[:], AF.Exp, scale=-1.0)
                return e.activation(EXL[:], LXS[:], AF.Exp)
            tok_sh = p.step("scalar", s_h, waits=[tok_vh, last["s2t"]], inc=sS)

            def v_sh(e):
                vbar(e, e.tensor_scalar(QH[:], QH[:], 127.0, None, OP.mult))
                vbar(e, e.tensor_tensor(BCOL[:], EXL[:],
                                        bass.AP(RSC2, 0, [[1, 128], [0, 2]]), OP.mult))
                return e.tensor_copy(QH[:], QH[:])
            tok_sh = p.step("vector", v_sh, waits=[tok_sh], inc=sV)
            last["sh"] = tok_sh
            yield

            def t_b(e):
                e.transpose(PTR[0:1, 0:128], BCOL[:, 0:1], IDF[:])
                return e.transpose(PTR[0:1, 128:256], BCOL[:, 1:2], IDF[:])

            def t_bb_fn(e):
                return e.matmul(PM1[:, 256:512], ONESF[0:1, :], QR1[0:1, 0:256],
                                start=True, stop=True)

            tok_tb = p.step("tensor", t_b, waits=[tok_sh, last["ptr_free"]], inc=sT)
            tok_vb = p.step("vector", lambda e: e.tensor_copy(QR1[0:1, 0:256], PTR[0:1, 0:256]),
                            waits=[tok_tb], inc=sV)
            last["ptr_free"] = tok_vb

            tok_tbb = p.step("tensor", t_bb_fn, waits=[tok_vb, last["pm1_free"]], inc=sT)
            tok_vbb = p.step("vector", lambda e: e.tensor_copy(BETA[:], PM1[:, 256:512]),
                             waits=[tok_tbb], inc=sV)
            last["pm1_free"] = tok_vbb
            yield

            yield "stg"
            rtok = tok_sh
            for tau in range(NTAU):
                def s_r(e, tau=tau):
                    return e.activation(HTM[tau][:], HTM[tau][:], AF.Identity,
                                        scale=QH[:, tau:tau + 1], bias=C128[:, 0:1])
                tok_sr = p.step("scalar", s_r, waits=[rtok, gelu_tok], inc=sS)
                pw = last["pstg_free"] if tau == 0 else None

                def v_r(e, tau=tau):
                    for hh in range(2):
                        vp = HTM[tau][:, hh * 2048:(hh + 1) * 2048]
                        ci = CINT[:, 0:2048]
                        gb = flat2(LNB, 2048, NG * TT)
                        e.tensor_copy(ci, vp)
                        e.scalar_tensor_tensor(gb, vp, -0.5, ci, OP.add, OP.is_gt)
                        last_i = e.scalar_tensor_tensor(
                            PSTG[:, tau, hh * 2048:(hh + 1) * 2048], ci, -128.0, gb,
                            OP.add, OP.add)
                    return last_i
                rtok = p.step("vector", v_r, waits=[tok_sr, pw], inc=sV)
                yield
            if stage <= 4:
                res["tok"] = rtok
                return

            yield "mm2"
            w2tok = last["w2s"]
            mm2_tok = None
            evh_toks = [None] * NH
            wd_toks = [None] * NH

            def emit_trh(m):
                mi = m % 2

                def t_trh(e, m=m, mi=mi):
                    for tau in range(NTAU):
                        last_i = e.transpose(
                            PTB[:, mi * 512 + tau * 128:mi * 512 + (tau + 1) * 128],
                            PSTG[:, tau, m * 128:(m + 1) * 128], IDB[:])
                    return last_i
                tok_trh = p.step("tensor", t_trh, waits=[rtok, last["ptb_free"]], inc=sT)
                if m == NH - 1:
                    last["pstg_free"] = tok_trh

                def v_evh(e, mi=mi):
                    return e.copy(XQHF[:, mi, :],
                                  PTB[:, mi * 512:mi * 512 + TT])
                tok_evh = p.step("scalar", v_evh, waits=[tok_trh], inc=sS)
                last["ptb_free"] = tok_evh
                evh_toks[m] = tok_evh

            def emit_mm2(m):
                nonlocal mm2_tok
                mi = m % 2
                wb = m % 2

                def t_mm2(e, m=m, mi=mi, wb=wb):
                    for d in range(NG):
                        last_i = e.matmul(PKVR[:, d * TT:(d + 1) * TT],
                                          W2S[wb][:, d * 128:(d + 1) * 128],
                                          XQHF[:, mi, :],
                                          start=(m == 0 and d % 2 == 0),
                                          stop=(m == NH - 1 and d % 2 == 1))
                    return last_i
                mw = [evh_toks[m], wd_toks[m]]
                if m == 0:
                    mw.append(last["pkvr_free"])
                mm2_tok = p.step("tensor", t_mm2, waits=mw, inc=sT)
                w2tok[m % 2] = mm2_tok

            if stage <= 4.5:
                for m in range(NH):
                    emit_trh(m)
                mm2_tok = evh_toks[-1]
            else:
                for m in range(NH):
                    wd_toks[m] = p.dma("sync", [(W2S[m % 2][:], w2t[:, m, :])], sW,
                                       waits=[w2tok[m % 2], last["w2t"]])
                    emit_trh(m)
                    if m >= 1:
                        emit_mm2(m - 1)
                    if m % 8 == 7:
                        yield
                emit_mm2(NH - 1)
            last["w2s"] = w2tok
            if stage <= 4.5:
                res["tok"] = mm2_tok
                return
            if stage <= 4.8:
                tok_dummy = p.step("vector",
                                   lambda e: e.tensor_copy(ABSW[:], v3(PKVR, 2048)),
                                   waits=[mm2_tok], inc=sV)
                last["pkvr_free"] = tok_dummy
                res["tok"] = tok_dummy
                return
            if stage <= 4.9:
                _tb = p.step("tensor", t_b, waits=[tok_sh, last["ptr_free"]], inc=sT)
                _vb = p.step("vector", lambda e: e.tensor_copy(QR1[0:1, 0:256], PTR[0:1, 0:256]),
                             waits=[_tb], inc=sV)
                last["ptr_free"] = _vb
                _tbb = p.step("tensor", t_bb_fn, waits=[_vb, last["pm1_free"]], inc=sT)
                _vbb = p.step("vector", lambda e: e.tensor_copy(BETA[:], PM1[:, 256:512]),
                              waits=[_tbb], inc=sV)
                last["pm1_free"] = _vbb
                tok_dummy = p.step("vector",
                                   lambda e: e.tensor_copy(ABSW[:], v3(PKVR, 2048)),
                                   waits=[mm2_tok, _vbb], inc=sV)
                last["pkvr_free"] = tok_dummy
                res["tok"] = tok_dummy
                return

            def v_epi(e):
                pk = v3(PKVR, 2048)
                e.tensor_tensor(ABSW[:], pk, bc_row(BETA), OP.mult)
                for g in range(NG):
                    last_i = e.scalar_tensor_tensor(xv[:, g, :], ABSW[:, g, :],
                                                    B2H[:, g:g + 1], xv[:, g, :],
                                                    OP.add, OP.add)
                return last_i
            tok_epi = p.step("vector", v_epi, waits=[mm2_tok, tok_vbb, xtok], inc=sV)
            last["pkvr_free"] = tok_epi
            res["tok"] = tok_epi

        def gen_attn(c, nb, xtok, ln2tok_c, ln2tok_n, res):
            xv = XB[nb]
            lv = LN2[c % 3]
            lnx = LN2[(c + 1) % 3]
            TMP1f = KT
            TMP2f = VT

            def v_mix(e):
                e.tensor_tensor(TMP1f[:], lv[:], bc_col(AM), OP.mult)
                e.tensor_tensor(TMP2f[:, :, 1:TT], lv[:, :, 0:TT - 1],
                                bc_col(BM, inner=TT - 1), OP.mult)
                e.tensor_tensor(TMP2f[:, :, 0:1], HAL[:], bc_col(BM, inner=1), OP.mult)
                for g in range(4, NG):
                    e.scalar_tensor_tensor(TMP1f[:, g, 0:TT - 1], lv[:, g, 1:TT],
                                           CM8[:, g:g + 1], TMP1f[:, g, 0:TT - 1],
                                           OP.mult, OP.add)
                    nxt = lnx[:, g, 0:1] if c < NCH - 1 else ZCOL[:, g, :]
                    e.scalar_tensor_tensor(TMP1f[:, g, TT - 1:TT], nxt,
                                           CM8[:, g:g + 1], TMP1f[:, g, TT - 1:TT],
                                           OP.mult, OP.add)
                return e.tensor_tensor(XM[:], TMP1f[:], TMP2f[:], OP.add)
            tok_mix = p.step("vector", v_mix,
                             waits=[ln2tok_c, ln2tok_n, last["scan"]], inc=sV)
            tok_hal = p.step("vector", lambda e: e.tensor_copy(HAL[:], lv[:, :, TT - 1:TT]),
                             waits=[tok_mix], inc=sV)
            yield

            wkstok = last["wks"]
            evac_k = None
            evac_v = None
            prev_mm = None
            for pi, wsrc in enumerate([wkt, wvt, wrt]):
                for j in range(NG):
                    wb = (pi * NG + j) % 3
                    tok_wd = p.dma("sync", [(WKS[wb][:], wsrc[:, j, :])], sW,
                                   waits=[wkstok[wb], last["wkvrt"]])

                    def t_kvr(e, j=j, wb=wb):
                        for i in range(NG):
                            last_i = e.matmul(PKVR[:, i * TT:(i + 1) * TT],
                                              WKS[wb][:, i * 128:(i + 1) * 128],
                                              XM[:, j, :],
                                              start=(j == 0 and i % 2 == 0),
                                              stop=(j == NG - 1 and i % 2 == 1))
                        return last_i
                    mw = [tok_wd, tok_mix, prev_mm]
                    if pi == 0 and j == 0:
                        mw.append(last["pkvr_free"])
                    if pi == 1 and j == 0:
                        mw.append(evac_k)
                    if pi == 2 and j == 0:
                        mw.append(evac_v)
                    prev_mm = p.step("tensor", t_kvr, waits=mw, inc=sT)
                    wkstok[wb] = prev_mm
                if pi == 0:
                    evac_k = p.step("vector",
                                    lambda e: e.tensor_scalar_min(KT[:], v3(PKVR, 2048),
                                                                  KCLAMP),
                                    waits=[prev_mm], inc=sV)
                elif pi == 1:
                    evac_v = p.step("scalar", lambda e: e.copy(VT[:], v3(PKVR, 2048)),
                                    waits=[prev_mm], inc=sS)
                yield
            tok_rmm = prev_mm
            last["wks"] = wkstok

            tok_ke = p.step("scalar", lambda e: e.activation(KT[:], KT[:], AF.Exp),
                            waits=[evac_k], inc=sS)
            yield


            half_y = []
            sg_toks = []
            prev_half = tok_ke
            for hh in range(2):
                g0 = hh * 4
                KTh = KT[:, g0:g0 + 4, :]
                VTh = VT[:, g0:g0 + 4, :]
                rph = bass.AP(PKVR, g0 * TT, [[2048, 128], [TT, 4], [1, TT]])

                def v_mscan(e, g0=g0, KTh=KTh):
                    e.tensor_copy(MT[:, :, 0:1], CARM[:, g0:g0 + 4])
                    for g in range(4):
                        last_i = e.tensor_tensor_scan(MT[:, g, 1:TT + 1], KTh[:, g, :],
                                                      KTh[:, g, :], CARM[:, g0 + g:g0 + g + 1],
                                                      OP.max, OP.max)
                    return last_i
                tok_ms = p.step("vector", v_mscan, waits=[tok_ke, last["scan"], prev_half],
                                inc=sV)

                tok_g1 = p.step("vector",
                                lambda e, KTh=KTh: e.tensor_tensor(GT[:], KTh,
                                                                   MT[:, :, 1:TT + 1],
                                                                   OP.subtract),
                                waits=[tok_ms], inc=sV)
                tok_g2 = p.step("scalar", lambda e: e.activation(GT[:], GT[:], AF.Exp),
                                waits=[tok_g1], inc=sS)
                yield

                def v_lam(e, g0=g0):
                    for g in range(4):
                        last_i = e.scalar_tensor_tensor(TMP1[:, g, :], MT[:, g, 0:TT],
                                                        W8[:, g0 + g:g0 + g + 1],
                                                        MT[:, g, 1:TT + 1],
                                                        OP.add, OP.subtract)
                    return last_i
                tok_l1 = p.step("vector", v_lam, waits=[tok_ms], inc=sV)
                tok_l2 = p.step("scalar", lambda e: e.activation(TMP1[:], TMP1[:], AF.Exp),
                                waits=[tok_l1], inc=sS)
                yield

                def v_scans(e, g0=g0, KTh=KTh, VTh=VTh):
                    e.tensor_tensor(KTh, GT[:], VTh, OP.mult)
                    e.tensor_copy(AT[:, :, 0:1], CARA[:, g0:g0 + 4])
                    e.tensor_copy(BT[:, :, 0:1], CARB[:, g0:g0 + 4])
                    for g in range(4):
                        e.tensor_tensor_scan(AT[:, g, 1:TT + 1], TMP1[:, g, :], KTh[:, g, :],
                                             CARA[:, g0 + g:g0 + g + 1], OP.mult, OP.add)
                    for g in range(4):
                        last_i = e.tensor_tensor_scan(BT[:, g, 1:TT + 1], TMP1[:, g, :],
                                                      GT[:, g, :],
                                                      CARB[:, g0 + g:g0 + g + 1],
                                                      OP.mult, OP.add)
                    return last_i
                tok_sc = p.step("vector", v_scans, waits=[tok_l2, tok_g2], inc=sV)
                yield

                def v_post1(e, g0=g0, VTh=VTh):
                    for g in range(4):
                        e.tensor_scalar(GT[:, g, :], GT[:, g, :],
                                        U8E[:, g0 + g:g0 + g + 1], None, OP.mult)
                    e.tensor_tensor(VTh, GT[:], VTh, OP.mult)
                    for g in range(4):
                        e.tensor_scalar(TMP2[:, g, :], TMP1[:, g, :],
                                        EWN[:, g0 + g:g0 + g + 1], None, OP.mult)
                    e.tensor_tensor(TMP1[:], AT[:, :, 0:TT], TMP2[:], OP.mult)
                    e.tensor_tensor(TMP2[:], BT[:, :, 0:TT], TMP2[:], OP.mult)
                    e.tensor_tensor(TMP2[:], TMP2[:], GT[:], OP.add)
                    return e.tensor_tensor(TMP1[:], TMP1[:], VTh, OP.add)
                tok_p1 = p.step("vector", v_post1, waits=[tok_sc], inc=sV)

                tok_sgh = p.step("scalar",
                                 lambda e, rph=rph: e.activation(GT[:], rph, AF.Exp,
                                                                 scale=-1.0),
                                 waits=[tok_rmm, evac_v, tok_p1], inc=sS)
                tok_sg1 = p.step("vector",
                                 lambda e: e.tensor_scalar(GT[:], GT[:], 1.0, None, OP.add),
                                 waits=[tok_sgh], inc=sV)
                sg_toks.append(tok_sg1)
                tok_dn = p.step("vector",
                                lambda e: e.tensor_tensor(TMP2[:], TMP2[:], GT[:],
                                                          OP.mult),
                                waits=[tok_sg1], inc=sV)
                yield

                def s_rec(e):
                    e.activation(TMP2[:], TMP2[:], AF.Ln)
                    return e.activation(TMP2[:], TMP2[:], AF.Exp, scale=-1.0)
                tok_rc = p.step("scalar", s_rec, waits=[tok_dn], inc=sS)

                def v_y(e, hh=hh, g0=g0):
                    e.tensor_tensor(YB[:, g0:g0 + 4, :], TMP1[:], TMP2[:], OP.mult)
                    e.tensor_copy(CARA[:, g0:g0 + 4], AT[:, :, TT:TT + 1])
                    e.tensor_copy(CARB[:, g0:g0 + 4], BT[:, :, TT:TT + 1])
                    return e.tensor_copy(CARM[:, g0:g0 + 4], MT[:, :, TT:TT + 1])
                tok_y = p.step("vector", v_y, waits=[tok_rc], inc=sV)
                half_y.append(tok_y)
                prev_half = tok_y
                yield
            last["scan"] = half_y[-1]
            tok_y = half_y[-1]

            prev_mm = None
            for j in range(NG):
                wb = j % 3
                tok_wd = p.dma("sync", [(WKS[wb][:], wot[:, j, :])], sW,
                               waits=[wkstok[wb], last["wot"]])

                def t_wo(e, j=j, wb=wb):
                    for i in range(NG):
                        last_i = e.matmul(PKVR[:, i * TT:(i + 1) * TT],
                                          WKS[wb][:, i * 128:(i + 1) * 128],
                                          YB[:, j, :],
                                          start=(j == 0 and i % 2 == 0),
                                          stop=(j == NG - 1 and i % 2 == 1))
                    return last_i
                mw = [tok_wd, half_y[0] if j < 4 else half_y[1], prev_mm]
                if j == 0:
                    mw.extend(sg_toks)
                prev_mm = p.step("tensor", t_wo, waits=mw, inc=sT)
                wkstok[wb] = prev_mm
            yield

            tok_res = p.step("vector",
                             lambda e: e.tensor_tensor(xv[:], xv[:], v3(PKVR, 2048), OP.add),
                             waits=[prev_mm, xtok], inc=sV)
            last["pkvr_free"] = tok_res
            res["tok"] = tok_res

        def emit_out(c, nb, xtok):
            tok_o = None
            for tau in range(NTAU):
                tok_e = None
                for gq in range(2):
                    def t_to(e, tau=tau, gq=gq, nb=nb):
                        for gg in range(4):
                            g = gq * 4 + gg
                            last_i = e.transpose(PTR[:, gg * 128:(gg + 1) * 128],
                                                 XB[nb][:, g, tau * 128:(tau + 1) * 128],
                                                 IDF[:])
                        return last_i
                    tok_t = p.step("tensor", t_to,
                                   waits=[xtok, last["ptr_free"], last["tsg_free"], tok_o],
                                   inc=sT)
                    tok_e = p.step("vector",
                                   lambda e, gq=gq: e.tensor_copy(
                                       TSG[:, gq * 512:(gq + 1) * 512], PTR[:, 0:512]),
                                   waits=[tok_t], inc=sV)
                    last["ptr_free"] = tok_e
                tok_o = p.dma("sync", [(y_out[c * TT + tau * 128: c * TT + (tau + 1) * 128, :],
                                        TSG[:, :])], sO, waits=[tok_e])
                last["tsg_free"] = tok_o
            last["xb"][nb] = tok_o
            return tok_o

        x1tok = [None] * NCH
        ln2tok = [None] * NCH

        _STAGE = {"f_ln": 1, "f_q": 2, "f_mm1": 3, "f_hq": 4, "f_tr": 4.5,
                  "f_mm2": 4.8, "f_bb": 4.9}

        def gen_front(c):
            tok_in = emit_in(c)
            nb = c % 3
            yield "x0"
            if parts in ("full", "ffn1", "ffn1ln", "dbg_ffn1"):
                r = {}
                yield from gen_ffn(c, nb, tok_in, r)
                t1 = r["tok"]
            elif parts in _STAGE:
                r = {}
                yield from gen_ffn(c, nb, tok_in, r, stage=_STAGE[parts])
                t1 = r["tok"]
            else:
                t1 = tok_in
            x1tok[c] = t1
            if parts in ("full", "ffn1ln"):
                ln2tok[c] = emit_ln(c, XB[nb], LN2[c % 3], t1, use_pkvr=True)
            else:
                ln2tok[c] = t1

        def drive(g):
            for _ in g:
                pass

        def interleave(a, b):
            alive_a, alive_b, b_paused = True, True, False
            while alive_a or alive_b:
                if alive_a:
                    try:
                        next(a)
                    except StopIteration:
                        alive_a = False
                        if b_paused:
                            b_paused = False
                if alive_b and not b_paused:
                    try:
                        if next(b) == "mm2" and alive_a:
                            b_paused = True
                    except StopIteration:
                        alive_b = False

        def interleave_hold(a, b):
            alive_b, b_paused = b is not None, False
            while True:
                try:
                    next(a)
                except StopIteration:
                    break
                if alive_b and not b_paused:
                    try:
                        if next(b) == "mm2":
                            b_paused = True
                    except StopIteration:
                        alive_b = False
            return b if alive_b else None

        def gen_tail(c, nb, atok):
            rf = {}
            yield from gen_ffn(c, nb, atok, rf)
            emit_out(c, nb, rf["tok"])

        PRO_MARK = {"s1": "s1_done", "w1": "w1_done", "s2": "s2_done",
                    "stg": "stg_done", "mm2": "stg_done"}

        fronts = {}
        if parts == "full" and PUMP_FRONTS:
            fronts = {ci: gen_front(ci) for ci in range(min(N_PUMP, NCH))}
            flags = set()
            paused = {}
            P = gen_prologue()
            while True:
                try:
                    mk = next(P)
                except StopIteration:
                    break
                if mk:
                    flags.add(mk)
                for ci in fronts:
                    g = fronts[ci]
                    if paused.get(ci) and paused[ci] not in flags:
                        continue
                    paused[ci] = None
                    try:
                        v = next(g)
                    except StopIteration:
                        continue
                    if ci == 0:
                        req = ("stg_done" if v in F0_HOLD else PRO_MARK.get(v)) \
                            if v is not None else None
                    else:
                        req = "stg_done" if v is not None else None
                    if req and req not in flags:
                        paused[ci] = req
            drive(fronts.pop(0))
            if 1 in fronts:
                drive(fronts.pop(1))
            else:
                drive(gen_front(1))
        else:
            g0 = gen_front(0)
            if EARLY_IN0:
                next(g0)
            drive(gen_prologue())
            drive(g0)
            if NCH > 1:
                drive(gen_front(1))

        for c in range(NCH):
            nb = c % 3
            if parts == "full":
                ra = {}
                A = gen_attn(c, nb, x1tok[c], ln2tok[c],
                             ln2tok[c + 1] if c + 1 < NCH else ln2tok[c], ra)
                B = fronts.pop(c + 2, None)
                if B is None and c + 2 < NCH:
                    B = gen_front(c + 2)
                B = interleave_hold(A, B)
                F = gen_tail(c, nb, ra["tok"])
                if B is not None:
                    interleave(B, F)
                else:
                    drive(F)
            else:
                t3 = x1tok[c]
                if c + 2 < NCH:
                    drive(gen_front(c + 2))
                emit_out(c, nb, t3)

        if parts.startswith("dbg"):
            dbg_outs = []
            def dump(name, ap, shape, dt=F32):
                o = nc.dram_tensor("dbg_" + name, shape, dt, kind="ExternalOutput").ap()
                dbg_outs.append((o, ap))
            dump("xqy", XQY[:], [128, NG, TT], BF16)
            dump("htm0", HTM[0][:], [128, H])
            dump("acol", ACOL[:], [128, 2])
            dump("qcol", QCOL[:], [128, 2])
            dump("xsc", XSC[:], [128, 2])
            dump("hsc", HSC[:], [128, 2])
            dump("qh", QH[:], [128, 2])
            dump("rsbeta", RS[:], [128, TT])
            dump("lnb", LNB[:], [128, NG, TT])
            dump("pstg0", PSTG[:, 0, :], [128, H], BF16)
            dump("w1s0", W1S[0][:], [128, NG, 512], BF16)
            dump("ls1", LS1[:], [128, 1])
            dump("rs1", RS1[:], [128, 1])
            dump("irb", IRB[0:1, 0, 0:256], [1, 256], BF16)
            dump("absw", ABSW[:], [128, NG, TT])
            tok_dbg = p.dma("gpsimd", [(o, ap) for (o, ap) in dbg_outs], sI,
                            waits=[last["scan"], last["pkvr_free"], last["xb"][0],
                                   last["xb"][1]])
            p.streams["gpsimd"].append(lambda e: e.wait_ge(sI.h, sI.n))
        final_o = sO.n
        p.streams["sync"].append(lambda e: e.wait_ge(sO.h, final_o))

        with nc.Block() as block:
            block.sync(lambda e: [fn(e) for fn in p.streams["sync"]])
            block.tensor(lambda e: [fn(e) for fn in p.streams["tensor"]])
            block.vector(lambda e: [fn(e) for fn in p.streams["vector"]])
            block.scalar(lambda e: [fn(e) for fn in p.streams["scalar"]])
            block.gpsimd(lambda e: [fn(e) for fn in p.streams["gpsimd"]])

    return nc


def kernel(L0_attn_x, L0_attn_time_decay, L0_attn_time_first, L0_attn_time_mix,
           L0_attn_combined_mix, L0_attn_Wk, L0_attn_Wv, L0_attn_Wr, L0_attn_Wo,
           L0_ffn_W1, L0_ffn_b1, L0_ffn_W2, L0_ffn_b2):
    x = np.ascontiguousarray(np.asarray(L0_attn_x, dtype=np.float32))
    args = {
        "td": np.ascontiguousarray(np.asarray(L0_attn_time_decay, np.float32)),
        "tf": np.ascontiguousarray(np.asarray(L0_attn_time_first, np.float32)),
        "tm": np.ascontiguousarray(np.asarray(L0_attn_time_mix, np.float32)),
        "cm": np.ascontiguousarray(np.asarray(L0_attn_combined_mix, np.float32)),
        "wk": np.ascontiguousarray(np.asarray(L0_attn_Wk, np.float32)),
        "wv": np.ascontiguousarray(np.asarray(L0_attn_Wv, np.float32)),
        "wr": np.ascontiguousarray(np.asarray(L0_attn_Wr, np.float32)),
        "wo": np.ascontiguousarray(np.asarray(L0_attn_Wo, np.float32)),
        "w1": np.ascontiguousarray(np.asarray(L0_ffn_W1, np.float32)),
        "b1": np.ascontiguousarray(np.asarray(L0_ffn_b1, np.float32)),
        "w2": np.ascontiguousarray(np.asarray(L0_ffn_W2, np.float32)),
        "b2": np.ascontiguousarray(np.asarray(L0_ffn_b2, np.float32)),
    }
    nc = build_program(PARTS)
    in_maps = [dict(args, x=np.ascontiguousarray(x[b])) for b in range(8)]
    res = run_bass_kernel_spmd(nc, in_maps, core_ids=list(range(8)))
    return np.stack([np.asarray(r["y"]) for r in res.results], axis=0)


# version 18
# speedup vs baseline: 1.0924x; 1.0030x over previous
import numpy as np
from contextlib import ExitStack

import concourse.bass as bass
import concourse.mybir as mybir
from concourse.bass_utils import run_bass_kernel_spmd

F32 = mybir.dt.float32
BF16 = mybir.dt.bfloat16
I16 = mybir.dt.int16
OP = mybir.AluOpType
AF = mybir.ActivationFunctionType
AX = mybir.AxisListType.X

T, C, H = 4096, 1024, 4096
NG = C // 128
NH = H // 128
TT = 256
NCH = T // TT
NTAU = TT // 128
QB = 127.0
EPS = 1e-5
KCLAMP = 60.0
LN127 = float(np.log(127.0))

PUMP_FRONTS = True
F0_HOLD = ()
N_PUMP = 1
EARLY_IN0 = False
PARTS = "full"


class _Sem:
    def __init__(self, h):
        self.h = h
        self.n = 0


class _Prog:
    def __init__(self, nc, ctx):
        self.nc = nc
        self.streams = {k: [] for k in ("tensor", "vector", "scalar", "sync", "gpsimd")}
        self.ctx = ctx

    def sem(self, name):
        return _Sem(self.ctx.enter_context(self.nc.semaphore(name)))

    def step(self, eng, fn, waits=(), inc=None, amt=1):
        wl = [(w[0].h, w[1]) for w in waits if w is not None]
        tok = None
        if inc is not None:
            inc.n += amt
            tok = (inc, inc.n)

        def run(e, wl=wl, fn=fn, inc=inc, amt=amt):
            for (h, v) in wl:
                e.wait_ge(h, v)
            last = fn(e)
            if inc is not None:
                assert last is not None, "step with inc must return last instruction"
                last.then_inc(inc.h, amt)
        self.streams[eng].append(run)
        return tok

    def dma(self, eng, pairs, sem, waits=()):
        wl = [(w[0].h, w[1]) for w in waits if w is not None]
        sem.n += 16 * len(pairs)
        tok = (sem, sem.n)

        def run(e, wl=wl, pairs=pairs, sem=sem):
            for (h, v) in wl:
                e.wait_ge(h, v)
            for (dst, src) in pairs:
                e.dma_start(dst, src).then_inc(sem.h, 16)
        self.streams[eng].append(run)
        return tok


def build_program(parts="full"):
    nc = bass.Bass("TRN2", target_bir_lowering=False, detect_race_conditions=False)

    x_in = nc.dram_tensor("x", [T, C], F32, kind="ExternalInput").ap()
    td_in = nc.dram_tensor("td", [C], F32, kind="ExternalInput").ap()
    tf_in = nc.dram_tensor("tf", [C], F32, kind="ExternalInput").ap()
    tm_in = nc.dram_tensor("tm", [C], F32, kind="ExternalInput").ap()
    cm_in = nc.dram_tensor("cm", [C], F32, kind="ExternalInput").ap()
    wk_in = nc.dram_tensor("wk", [C, C], F32, kind="ExternalInput").ap()
    wv_in = nc.dram_tensor("wv", [C, C], F32, kind="ExternalInput").ap()
    wr_in = nc.dram_tensor("wr", [C, C], F32, kind="ExternalInput").ap()
    wo_in = nc.dram_tensor("wo", [C, C], F32, kind="ExternalInput").ap()
    w1_in = nc.dram_tensor("w1", [H, C], F32, kind="ExternalInput").ap()
    b1_in = nc.dram_tensor("b1", [H], F32, kind="ExternalInput").ap()
    w2_in = nc.dram_tensor("w2", [C, H], F32, kind="ExternalInput").ap()
    b2_in = nc.dram_tensor("b2", [C], F32, kind="ExternalInput").ap()
    y_out = nc.dram_tensor("y", [T, C], F32, kind="ExternalOutput").ap()

    w1t = nc.dram_tensor("w1t", [128, 8, 8, 512], BF16, kind="Internal").ap()
    w2t = nc.dram_tensor("w2t", [128, NH, C], BF16, kind="Internal").ap()
    wkt = nc.dram_tensor("wkt", [128, NG, C], BF16, kind="Internal").ap()
    wvt = nc.dram_tensor("wvt", [128, NG, C], BF16, kind="Internal").ap()
    wrt = nc.dram_tensor("wrt", [128, NG, C], BF16, kind="Internal").ap()
    wot = nc.dram_tensor("wot", [128, NG, C], BF16, kind="Internal").ap()

    with ExitStack() as ctx:
        p = _Prog(nc, ctx)
        sb = lambda name, shape, dt=F32: ctx.enter_context(nc.sbuf_tensor(name, shape, dt))
        ps_ = lambda name, shape, dt=F32: ctx.enter_context(nc.psum_tensor(name, shape, dt))

        W8 = sb("W8", [128, NG]); EWN = sb("EWN", [128, NG]); U8E = sb("U8E", [128, NG])
        AM = sb("AM", [128, NG]); BM = sb("BM", [128, NG]); CM8 = sb("CM8", [128, NG])
        B2H = sb("B2H", [128, NG])
        VG = sb("VG", [32, 128])
        CEPS = sb("CEPS", [128, 1]); CL127 = sb("CL127", [128, 1]); C128 = sb("C128", [128, 1])
        LS1 = sb("LS1", [128, 1]); LB2S = sb("LB2S", [128, 1])
        RS1 = sb("RS1", [128, 1]); RS2 = sb("RS2", [128, 1])
        TH1 = sb("TH1", [128, 1]); TH2 = sb("TH2", [128, 1])
        RSC1 = sb("RSC1", [128, 1]); RSC2 = sb("RSC2", [128, 1])
        EXL = sb("EXL", [128, 2])
        SACC = sb("SACC", [128, 1]); SRED = sb("SRED", [128, 1])
        ONESF = sb("ONESF", [128, 128]); ONESB = sb("ONESB", [128, 128], BF16)
        IDF = sb("IDF", [128, 128]); IDB = sb("IDB", [128, 128], BF16)
        B1F = sb("B1F", [128, NH])

        XB = [sb(f"XB{i}", [128, NG, TT]) for i in range(3)]
        TSG = sb("TSG", [128, C])
        LNB = sb("LNB", [128, NG, TT])
        LN2 = [sb(f"LN2_{i}", [128, NG, TT], BF16) for i in range(3)]
        HAL = sb("HAL", [128, NG, 1]); ZCOL = sb("ZCOL", [128, NG, 1])
        XM = sb("XM", [128, NG, TT], BF16)
        XMF = sb("XMF", [128, NG, TT], BF16)
        XQY = sb("XQY", [128, NG, TT], BF16)
        YB = sb("YB", [128, NG, TT], BF16)
        ABSW = sb("ABSW", [128, NG, TT])
        CINT = sb("CINT", [128, 2048], I16)
        KT = sb("KT", [128, NG, TT]); VT = sb("VT", [128, NG, TT])
        GT = sb("GT", [128, 4, TT])
        TMP1 = sb("TMP1", [128, 4, TT]); TMP2 = sb("TMP2", [128, 4, TT])
        MT = sb("MT", [128, 4, TT + 1]); AT = sb("AT", [128, 4, TT + 1])
        BT = sb("BT", [128, 4, TT + 1])
        CARM = sb("CARM", [128, NG]); CARA = sb("CARA", [128, NG]); CARB = sb("CARB", [128, NG])
        HTM = [sb(f"HTM{i}", [128, H]) for i in range(2)]
        PSTG = sb("PSTG", [128, 3, H], BF16)
        XQHF = sb("XQHF", [128, 2, TT], BF16)
        STRB = sb("STRB", [128, 512], BF16)
        W1S = [sb(f"W1S{i}", [128, NG, 512], BF16) for i in range(2)]
        W2S = [sb(f"W2S{i}", [128, C], BF16) for i in range(2)]
        WKS = [sb(f"WKS{i}", [128, C], BF16) for i in range(3)]
        MS = sb("MS", [128, TT]); RS = sb("RS", [128, TT])
        XSC = sb("XSC", [128, 2]); LXS = sb("LXS", [128, 2])
        QCOL = sb("QCOL", [128, 2]); ACOL = sb("ACOL", [128, 2])
        HSC = sb("HSC", [128, 2]); QH = sb("QH", [128, 2]); BCOL = sb("BCOL", [128, 2])
        QR1 = sb("QR1", [1, TT])
        IRROW = sb("IRROW", [1, TT], BF16)
        BETA = sb("BETA", [128, TT])

        PM1 = ps_("PM1", [128, 1024])
        PKVR = ps_("PKVR", [128, 2048])
        PTR = ps_("PTR", [128, 512])
        PTB = ps_("PTB", [128, 1024], BF16)

        sT = p.sem("sT"); sV = p.sem("sV"); sS = p.sem("sS"); sG = p.sem("sG")
        sW = p.sem("sW"); sI = p.sem("sI"); sO = p.sem("sO")
        sBV = p.sem("sBV"); sBS = p.sem("sBS")

        def vbar(e, inst):
            sBV.n += 1
            inst.then_inc(sBV.h, 1)
            e.wait_ge(sBV.h, sBV.n)

        def sbar(e, inst):
            sBS.n += 1
            inst.then_inc(sBS.h, 1)
            e.wait_ge(sBS.h, sBS.n)

        def v3(h, free_total, n=NG, inner=TT, off=0):
            return bass.AP(h, off, [[free_total, 128], [inner, n], [1, inner]])

        def bc_row(h, inner=TT, n=NG, stride=None, off=0):
            return bass.AP(h, off, [[stride if stride is not None else inner, 128],
                                    [0, n], [1, inner]])

        def bc_col(h, n=NG, inner=TT):
            return bass.AP(h, 0, [[n, 128], [1, n], [0, inner]])

        def flat2(h, ncols, free_total, off=0):
            return bass.AP(h, off, [[free_total, 128], [1, ncols]])

        def g_init(e):
            e.memset(ONESF[:], 1.0)
            e.memset(ONESB[:], 1.0)
            e.memset(IDF[:], 0.0)
            e.affine_select(out=IDF[:], in_=IDF[:], compare_op=OP.not_equal,
                            fill=1.0, base=0, pattern=[[-1, 128]], channel_multiplier=1)
            e.memset(IDB[:], 0.0)
            e.affine_select(out=IDB[:], in_=IDB[:], compare_op=OP.not_equal,
                            fill=1.0, base=0, pattern=[[-1, 128]], channel_multiplier=1)
            e.memset(CEPS[:], EPS)
            e.memset(CL127[:], LN127)
            e.memset(C128[:], 128.0)
            e.memset(ZCOL[:], 0.0)
            e.memset(HAL[:], 0.0)
            e.memset(CARA[:], 0.0)
            e.memset(CARB[:], 0.0)
            return e.memset(CARM[:], -1e38)
        tok_ginit = p.step("gpsimd", g_init, inc=sG)

        tok_b1r = p.dma("gpsimd", [(PSTG[0:1, 2, 0:H], b1_in[None, :])], sI,
                        waits=[tok_ginit])

        vecs = [(td_in, W8), (tf_in, U8E), (tm_in, AM), (cm_in, CM8), (b2_in, B2H)]
        prev_tok = tok_ginit
        for (vin, vdst) in vecs:
            tok_l = p.dma("sync", [(VG[0:8, :], vin.rearrange("(g p) -> g p", p=128))],
                          sW, waits=[prev_tok])
            tok_t = p.step("tensor", lambda e: e.transpose(PTR[:, 0:8], VG[0:8, :], IDF[0:8, 0:8]),
                           waits=[tok_l, tok_ginit, prev_tok], inc=sT)
            prev_tok = p.step("vector", lambda e, vd=vdst: e.tensor_copy(vd[:], PTR[:, 0:8]),
                              waits=[tok_t], inc=sV)
        tok_params = prev_tok

        tok_b1l = p.dma("sync", [(VG[0:32, :], b1_in.rearrange("(g p) -> g p", p=128))],
                        sW, waits=[tok_params])
        tok_b1t = p.step("tensor", lambda e: e.transpose(PTR[:, 0:32], VG[0:32, :], IDF[0:32, 0:32]),
                         waits=[tok_b1l], inc=sT)
        tok_b1 = p.step("vector", lambda e: e.tensor_copy(B1F[:], PTR[:, 0:32]),
                        waits=[tok_b1t], inc=sV)

        def v_derived(e):
            e.tensor_scalar(W8[:], W8[:], 1.0 / T, None, OP.mult)
            vbar(e, e.tensor_scalar(BM[:], AM[:], -1.0, 1.0, OP.mult, OP.add))
            e.tensor_tensor(BM[:, 0:4], BM[:, 0:4], CM8[:, 0:4], OP.add)
            return e.tensor_scalar(B2H[:], B2H[:], 0.5, None, OP.mult)
        tok_vd = p.step("vector", v_derived, waits=[tok_params, tok_b1], inc=sV)

        def s_derived(e):
            e.activation(EWN[:], W8[:], AF.Exp, scale=-1.0)
            return e.activation(U8E[:], U8E[:], AF.Exp, scale=1.0 / T)
        tok_sd = p.step("scalar", s_derived, waits=[tok_vd], inc=sS)
        tok_base = tok_sd

        def q_round(e, vp, ci, gb):
            e.tensor_copy(ci, vp)
            e.scalar_tensor_tensor(gb, vp, -0.5, ci, OP.add, OP.is_gt)
            return e.scalar_tensor_tensor(vp, ci, -128.0, gb, OP.add, OP.add)

        last = {
            "pm1_free": tok_base, "pkvr_free": tok_base,
            "ptr_free": tok_b1, "ptb_free": tok_base,
            "xb": [tok_base, tok_base, tok_base], "tsg_free": tok_base,
            "w1s": [tok_base, tok_base], "w2s": [tok_base, tok_base],
            "wks": [tok_base, tok_base, tok_base], "scan": tok_base,
            "sxs": None, "sh": None, "ln_busy": tok_base, "pstg_free": tok_base,
            "s1t": tok_base, "s2t": tok_base, "b1r": tok_b1r,
            "w1t": tok_base, "w2t": tok_base, "wkvrt": tok_base, "wot": tok_base,
        }

        def KHf(q):
            return flat2(KT if q < 2 else VT, C, NG * TT, off=(q % 2) * C)

        def KH2f(b):
            return flat2(KT if b == 0 else VT, 2 * C, NG * TT)

        def gen_prologue():
            tok_zacc = p.step("vector", lambda e: e.memset(SACC[:], 0.0), inc=sV)
            rtoks = [tok_zacc] * 4
            for t in range(32):
                q = t % 4
                tok_l = p.dma("sync", [(KHf(q), w1_in[t * 128:(t + 1) * 128, :])],
                              sW, waits=[rtoks[q]])

                def v_abs(e, q=q):
                    vbar(e, e.tensor_reduce(SRED[:, 0:1], KHf(q), AX,
                                            OP.add, apply_absolute_value=True))
                    vbar(e, e.tensor_tensor(SACC[:], SACC[:], SRED[:], OP.add))
                    return e.tensor_copy(SACC[:], SACC[:])
                rtoks[q] = p.step("vector", v_abs, waits=[tok_l], inc=sV)
                if t % 8 == 7:
                    yield
            tok_s1mm = p.step("tensor", lambda e: e.matmul(PKVR[:, 0:1], ONESF[:],
                                                           SACC[:, 0:1],
                                                           start=True, stop=True),
                              waits=[rtoks[3], last["pkvr_free"]], inc=sT)

            def v_s1(e):
                vbar(e, e.tensor_scalar(SRED[:], PKVR[:, 0:1], 1.0 / (H * C), None, OP.mult))
                vbar(e, e.tensor_scalar_max(SRED[:], SRED[:], EPS))
                e.tensor_scalar(RSC1[:], SRED[:], 1.0 / 127.0, None, OP.mult)
                return e.tensor_scalar(TH1[:], SRED[:], 0.5, None, OP.mult)
            tok_vs1 = p.step("vector", v_s1, waits=[tok_s1mm], inc=sV)
            last["pkvr_free"] = tok_vs1

            def s_s1(e):
                e.activation(LS1[:], SRED[:], AF.Ln, scale=1.0 / 127.0)
                sbar(e, e.activation(RS1[:], SRED[:], AF.Ln))
                return e.activation(RS1[:], RS1[:], AF.Exp, scale=-1.0)
            tok_ss1 = p.step("scalar", s_s1, waits=[tok_vs1], inc=sS)
            last["s1t"] = tok_ss1
            yield "s1_done"

            tB_done = tok_ss1
            prev_dma = None
            for hs in range(8):
                tok_l = p.dma("sync", [(KHf(i), w1_in[(hs * 4 + i) * 128:
                                                      (hs * 4 + i + 1) * 128, :])
                                       for i in range(4)], sW, waits=[tB_done])

                def v_q(e):
                    thb = bass.AP(TH1, 0, [[1, 128], [0, C]])
                    for i in range(4):
                        vp = KHf(i)
                        gb = flat2(GT, C, 4 * TT)
                        e.scalar_tensor_tensor(gb, vp, 1.0, thb, OP.mult, OP.is_gt)
                        e.scalar_tensor_tensor(vp, vp, -1.0, thb, OP.mult, OP.is_gt)
                        last_i = e.tensor_tensor(PSTG[:, 0, i * C:(i + 1) * C],
                                                 gb, vp, OP.subtract)
                    return last_i
                tok_vq = p.step("vector", v_q, waits=[tok_l, tok_ss1], inc=sV)

                for j in range(8):
                    def t_tr(e, j=j):
                        for i in range(4):
                            last_i = e.transpose(PTB[:, i * 128:(i + 1) * 128],
                                                 PSTG[:, 0, i * C + j * 128: i * C + (j + 1) * 128],
                                                 IDB[:])
                        return last_i
                    tok_tr = p.step("tensor", t_tr, waits=[tok_vq, tok_ginit, prev_dma], inc=sT)
                    tok_ev = p.step("vector", lambda e: e.tensor_copy(STRB[:, 0:512], PTB[:, 0:512]),
                                    waits=[tok_tr, prev_dma], inc=sV)
                    prev_dma = p.dma("sync", [(w1t[:, hs, j, :], STRB[:, 0:512])], sW,
                                     waits=[tok_ev])
                tB_done = prev_dma
                yield
            last["w1t"] = prev_dma
            yield "w1_done"

            tok_zacc2 = p.step("vector", lambda e: e.memset(SACC[:], 0.0),
                               waits=[tB_done, tok_ss1, tok_s1mm], inc=sV)
            rtoks2 = [tok_zacc2, tok_zacc2]
            for it in range(16):
                bi = it % 2
                tok_l = p.dma("sync", [(KH2f(bi), w2_in.rearrange("d (a h) -> d a h", a=2)
                                        [it // 2 * 128:(it // 2 + 1) * 128, it % 2, :])],
                              sW, waits=[rtoks2[bi]])

                def v_abs2(e, bi=bi):
                    vbar(e, e.tensor_reduce(SRED[:, 0:1], KH2f(bi), AX, OP.add,
                                            apply_absolute_value=True))
                    vbar(e, e.tensor_tensor(SACC[:], SACC[:], SRED[:], OP.add))
                    return e.tensor_copy(SACC[:], SACC[:])
                rtoks2[bi] = p.step("vector", v_abs2, waits=[tok_l], inc=sV)
                if it % 4 == 3:
                    yield
            tok_s2mm = p.step("tensor", lambda e: e.matmul(PKVR[:, 0:1], ONESF[:],
                                                           SACC[:, 0:1],
                                                           start=True, stop=True),
                              waits=[rtoks2[0], rtoks2[1], last["pkvr_free"]], inc=sT)

            def v_s2(e):
                vbar(e, e.tensor_scalar(SRED[:], PKVR[:, 0:1], 1.0 / (H * C), None, OP.mult))
                vbar(e, e.tensor_scalar_max(SRED[:], SRED[:], EPS))
                e.tensor_scalar(RSC2[:], SRED[:], 0.5 / 127.0, None, OP.mult)
                return e.tensor_scalar(TH2[:], SRED[:], 0.5, None, OP.mult)
            tok_vs2 = p.step("vector", v_s2, waits=[tok_s2mm], inc=sV)
            last["pkvr_free"] = tok_vs2

            def s_s2(e):
                e.activation(LB2S[:], SRED[:], AF.Ln, scale=0.5 / 127.0)
                sbar(e, e.activation(RS2[:], SRED[:], AF.Ln))
                return e.activation(RS2[:], RS2[:], AF.Exp, scale=-1.0)
            tok_ss2 = p.step("scalar", s_s2, waits=[tok_vs2], inc=sS)
            last["s2t"] = tok_ss2
            yield "s2_done"

            tB2 = tok_ss2
            for quar in range(4):
                for i in range(2):
                    dti = quar * 2 + i
                    tok_l = p.dma("sync", [(KH2f(a), w2_in.rearrange("d (a h) -> d a h", a=2)
                                            [dti * 128:(dti + 1) * 128, a, :])
                                           for a in range(2)],
                                  sW, waits=[tB2])

                    def v_q2(e, i=i):
                        thb = bass.AP(TH2, 0, [[1, 128], [0, C]])
                        for q in range(4):
                            vp = KHf(q)
                            gb = flat2(GT, C, 4 * TT)
                            e.scalar_tensor_tensor(gb, vp, 1.0, thb, OP.mult, OP.is_gt)
                            e.scalar_tensor_tensor(vp, vp, -1.0, thb, OP.mult, OP.is_gt)
                            last_i = e.tensor_tensor(PSTG[:, i, q * C:(q + 1) * C],
                                                     gb, vp, OP.subtract)
                        return last_i
                    tB2 = p.step("vector", v_q2, waits=[tok_l, tok_ss2], inc=sV)
                for m in range(NH):
                    def t_tr2(e, m=m):
                        for i in range(2):
                            last_i = e.transpose(PTB[:, i * 128:(i + 1) * 128],
                                                 PSTG[:, i, m * 128:(m + 1) * 128], IDB[:])
                        return last_i
                    tok_tr = p.step("tensor", t_tr2, waits=[tB2, prev_dma], inc=sT)
                    tok_ev = p.step("vector", lambda e: e.tensor_copy(STRB[:, 0:256], PTB[:, 0:256]),
                                    waits=[tok_tr, prev_dma], inc=sV)
                    prev_dma = p.dma("sync", [(w2t[:, m, quar * 256:(quar + 1) * 256],
                                               STRB[:, 0:256])], sW, waits=[tok_ev])
                    if m % 8 == 7:
                        yield
                tB2 = prev_dma
            last["w2t"] = prev_dma

            tokAW = tB2
            final_tr = None
            for (win, wdst) in [(wk_in, wkt), (wv_in, wvt), (wr_in, wrt), (wo_in, wot)]:
                htoks = [tokAW, tokAW]
                ctoks = []
                for i in range(8):
                    bi = i % 2
                    tok_l = p.dma("sync", [(KHf(bi * 2), win[i * 128:(i + 1) * 128, :])], sW,
                                  waits=[htoks[bi]])
                    htoks[bi] = p.step("vector",
                                       lambda e, i=i, bi=bi: e.tensor_copy(
                                           PSTG[:, i // 4, (i % 4) * C:(i % 4 + 1) * C],
                                           KHf(bi * 2)),
                                       waits=[tok_l, tokAW], inc=sV)
                    ctoks.append(htoks[bi])
                for j in range(8):
                    def t_tr3(e, j=j):
                        for i in range(8):
                            last_i = e.transpose(
                                PTB[:, i * 128:(i + 1) * 128],
                                PSTG[:, i // 4, (i % 4) * C + j * 128:(i % 4) * C + (j + 1) * 128],
                                IDB[:])
                        return last_i
                    tok_tr = p.step("tensor", t_tr3, waits=ctoks[-2:] + [prev_dma], inc=sT)
                    final_tr = tok_tr

                    def v_ev3(e):
                        return e.tensor_copy(STRB[:, 0:512], PTB[:, 0:512])
                    tok_ev = p.step("vector", v_ev3, waits=[tok_tr, prev_dma], inc=sV)
                    half_dma = p.dma("sync", [(wdst[:, j, 0:512], STRB[:, 0:512])], sW,
                                     waits=[tok_ev])
                    tok_ev2 = p.step("vector",
                                     lambda e: e.tensor_copy(STRB[:, 0:512], PTB[:, 512:1024]),
                                     waits=[tok_tr, half_dma], inc=sV)
                    prev_dma = p.dma("sync", [(wdst[:, j, 512:1024], STRB[:, 0:512])], sW,
                                     waits=[tok_ev2])
                    if j % 4 == 3:
                        yield
                tokAW = prev_dma
                if wdst is wrt:
                    last["wkvrt"] = prev_dma
            last["wot"] = prev_dma
            last["pstg_free"] = final_tr
            last["ptb_free"] = tok_ev2
            last["scan"] = prev_dma
            yield "stg_done"

        def emit_in(c):
            nb = c % 3
            tok_e = None
            for tau in range(NTAU):
                tok_d = p.dma("gpsimd", [(TSG[:, :],
                                          x_in[c * TT + tau * 128: c * TT + (tau + 1) * 128, :])],
                              sI, waits=[last["tsg_free"], last["xb"][nb], tok_e])
                for gp in range(2):
                    def t_tin(e, gp=gp):
                        for gg in range(4):
                            g = gp * 4 + gg
                            last_i = e.transpose(PTR[:, gg * 128:(gg + 1) * 128],
                                                 TSG[:, g * 128:(g + 1) * 128], IDF[:])
                        return last_i
                    tok_t = p.step("tensor", t_tin,
                                   waits=[tok_d, last["ptr_free"], tok_e], inc=sT)
                    tok_e = p.step("scalar",
                                   lambda e, gp=gp, nb=nb, tau=tau: e.copy(
                                       XB[nb][:, gp * 4:gp * 4 + 4, tau * 128:(tau + 1) * 128],
                                       PTR[:, 0:512]),
                                   waits=[tok_t], inc=sS)
                    last["ptr_free"] = tok_e
                last["tsg_free"] = tok_e
            last["xb"][nb] = tok_e
            return tok_e

        def emit_ln(c, xv, out_bf, xtok, use_pkvr=False):
            PS = PKVR if use_pkvr else PM1
            freek = "pkvr_free" if use_pkvr else "pm1_free"
            tok_bf = p.step("scalar", lambda e: e.copy(XMF[:], xv[:]),
                            waits=[xtok, last["ln_busy"]], inc=sS)

            def t_sumA(e):
                for g in range(NG):
                    last_i = e.matmul(PS[:, 0:TT], ONESB[:], XMF[:, g, :],
                                      start=(g == 0), stop=(g == NG - 1))
                return last_i
            tok_mmA = p.step("tensor", t_sumA, waits=[tok_bf, last[freek]], inc=sT)
            tok_sq = p.step("scalar", lambda e: e.activation(XMF[:], xv[:], AF.Square),
                            waits=[xtok, tok_mmA], inc=sS)

            def t_sumB(e):
                for g in range(NG):
                    last_i = e.matmul(PS[:, TT:2 * TT], ONESB[:], XMF[:, g, :],
                                      start=(g == 0), stop=(g == NG - 1))
                return last_i
            tok_mm = p.step("tensor", t_sumB, waits=[tok_sq], inc=sT)

            def v_stats(e):
                e.tensor_scalar(MS[:], PS[:, 0:TT], 1.0 / C, None, OP.mult)
                e.tensor_scalar(RS[:], PS[:, TT:2 * TT], 1.0 / C, None, OP.mult)
                e.tensor_tensor(LNB[:, 0, :], MS[:], MS[:], OP.mult)
                return e.tensor_tensor(RS[:], RS[:], LNB[:, 0, :], OP.subtract)
            tok_vs = p.step("vector", v_stats, waits=[tok_mm], inc=sV)
            last[freek] = tok_vs

            def s_stats(e):
                e.activation(RS[:], RS[:], AF.Ln, bias=CEPS[:, 0:1])
                return e.activation(RS[:], RS[:], AF.Exp, scale=-0.5)
            tok_ss = p.step("scalar", s_stats, waits=[tok_vs], inc=sS)

            dst = out_bf if out_bf is not None else LNB

            def v_norm(e):
                e.tensor_tensor(MS[:], MS[:], RS[:], OP.mult)
                e.tensor_tensor(LNB[:], xv[:], bc_row(RS), OP.mult)
                return e.tensor_tensor(dst[:], LNB[:], bc_row(MS), OP.subtract)
            tok_vn = p.step("vector", v_norm, waits=[tok_ss, xtok], inc=sV)
            last["ln_busy"] = tok_vn
            return tok_vn

        def gen_ffn(c, nb, xtok, res, stage=5):
            xv = XB[nb]
            tok_ln = emit_ln(c, xv, None, xtok)
            yield "s1"
            if stage <= 1:
                res["tok"] = tok_ln
                return

            def v_abs(e):
                e.scalar_tensor_tensor(ABSW[:], LNB[:], -1.0, LNB[:], OP.mult, OP.max)
                for g in range(1, NG):
                    last_i = e.tensor_tensor(ABSW[:, 0, :], ABSW[:, 0, :], ABSW[:, g, :],
                                             OP.max)
                return last_i
            tok_va = p.step("vector", v_abs, waits=[tok_ln], inc=sV)

            def t_trx(e):
                for tau in range(NTAU):
                    last_i = e.transpose(PTR[:, tau * 128:(tau + 1) * 128],
                                         ABSW[:, 0, tau * 128:(tau + 1) * 128], IDF[:])
                return last_i
            tok_tt = p.step("tensor", t_trx, waits=[tok_va, last["ptr_free"]], inc=sT)

            def v_xs(e):
                for tau in range(NTAU):
                    vbar(e, e.tensor_reduce(XSC[:, tau:tau + 1],
                                            PTR[:, tau * 128:(tau + 1) * 128], AX, OP.max))
                return e.tensor_scalar_max(XSC[:], XSC[:], EPS)
            tok_vxs = p.step("vector", v_xs, waits=[tok_tt, last["sxs"]], inc=sV)
            last["ptr_free"] = tok_vxs

            def s_xs(e):
                sbar(e, e.activation(LXS[:], XSC[:], AF.Ln))
                e.activation(QCOL[:], LXS[:], AF.Exp, scale=-1.0)
                return e.activation(EXL[:], LXS[:], AF.Exp)
            tok_sxs = p.step("scalar", s_xs,
                             waits=[tok_vxs, last["s1t"], last["sh"]], inc=sS)

            def v_sc(e):
                e.tensor_scalar(QCOL[:], QCOL[:], 127.0, None, OP.mult)
                vbar(e, e.tensor_tensor(ACOL[:], EXL[:],
                                        bass.AP(RSC1, 0, [[1, 128], [0, 2]]), OP.mult))
                return e.tensor_copy(ACOL[:], ACOL[:])
            tok_sxs = p.step("vector", v_sc, waits=[tok_sxs], inc=sV)
            last["sxs"] = tok_sxs
            yield

            def t_q(e):
                e.transpose(PTR[0:1, 0:128], QCOL[:, 0:1], IDF[:])
                return e.transpose(PTR[0:1, 128:256], QCOL[:, 1:2], IDF[:])
            tok_tq = p.step("tensor", t_q, waits=[tok_sxs, last["ptr_free"]], inc=sT)
            tok_vq = p.step("vector", lambda e: e.tensor_copy(QR1[0:1, 0:256], PTR[0:1, 0:256]),
                            waits=[tok_tq], inc=sV)
            last["ptr_free"] = tok_vq

            tok_vir = p.step("vector",
                             lambda e: e.tensor_tensor(IRROW[0:1, 0:256], QR1[0:1, 0:256],
                                                       bass.AP(RS1, 0, [[1, 1], [0, 256]]),
                                                       OP.mult),
                             waits=[tok_vq, last["s1t"]], inc=sV)

            def t_qb(e):
                return e.matmul(PM1[:, 256:512], ONESF[0:1, :], QR1[0:1, 0:256],
                                start=True, stop=True)
            tok_tqb = p.step("tensor", t_qb, waits=[tok_vq, last["pm1_free"]], inc=sT)

            def v_round_in(e):
                e.tensor_tensor(LNB[:], LNB[:], bc_row(PM1, stride=1024, off=256), OP.mult)
                e.tensor_scalar(LNB[:], LNB[:], 128.0, None, OP.add)
                cv = v3(CINT, 2048)
                e.tensor_copy(cv, LNB[:])
                e.scalar_tensor_tensor(ABSW[:], LNB[:], -0.5, cv, OP.add, OP.is_gt)
                return e.scalar_tensor_tensor(XQY[:], cv, -128.0, ABSW[:], OP.add, OP.add)
            tok_xq = p.step("vector", v_round_in, waits=[tok_tqb, tok_va], inc=sV)
            yield "w1"
            if stage <= 2:
                res["tok"] = tok_xq
                return

            half_evac = [tok_xq, tok_xq]
            w1tok = last["w1s"]
            w1dma = None
            gelu_tok = None
            for k in range(16):
                hs, tau = k // 2, k % 2
                wb = hs % 2
                if tau == 0:
                    w1dma = p.dma("sync", [(W1S[wb][:], w1t[:, hs])], sW,
                                  waits=[w1tok[wb], last["w1t"]])

                def t_mm1(e, hs=hs, tau=tau, wb=wb, k=k):
                    ph = PM1[:, (k % 2) * 512:(k % 2 + 1) * 512]
                    for cc in range(NG):
                        e.matmul(ph, XQY[:, cc, tau * 128:(tau + 1) * 128],
                                 W1S[wb][:, cc, :], start=(cc == 0), stop=False)
                    return e.matmul(ph, IRROW[0:1, tau * 128:(tau + 1) * 128],
                                    PSTG[0:1, 2, hs * 512:(hs + 1) * 512],
                                    start=False, stop=True)
                tok_m = p.step("tensor", t_mm1,
                               waits=[w1dma, tok_xq, tok_vir, half_evac[k % 2]] +
                                     ([last["b1r"]] if k == 0 else []), inc=sT)
                if tau == 1:
                    w1tok[wb] = tok_m

                def s_gelu(e, hs=hs, tau=tau, k=k):
                    return e.activation(HTM[tau][:, hs * 512:(hs + 1) * 512],
                                        PM1[:, (k % 2) * 512:(k % 2 + 1) * 512],
                                        AF.Gelu, scale=ACOL[:, tau:tau + 1])
                gelu_tok = p.step("scalar", s_gelu, waits=[tok_m, tok_sxs], inc=sS)
                half_evac[k % 2] = gelu_tok
                if k % 4 == 3:
                    yield
            last["pm1_free"] = gelu_tok
            last["w1s"] = w1tok
            yield "s2"
            if stage <= 3:
                res["tok"] = gelu_tok
                return

            def v_habs(e):
                for tau in range(NTAU):
                    vbar(e, e.tensor_reduce(HSC[:, tau:tau + 1], HTM[tau][:], AX, OP.max,
                                            apply_absolute_value=True))
                return e.tensor_scalar_max(HSC[:], HSC[:], EPS)
            tok_vh = p.step("vector", v_habs, waits=[gelu_tok, last["sh"]], inc=sV)

            def s_h(e):
                sbar(e, e.activation(LXS[:], HSC[:], AF.Ln))
                e.activation(QH[:], LXS[:], AF.Exp, scale=-1.0)
                return e.activation(EXL[:], LXS[:], AF.Exp)
            tok_sh = p.step("scalar", s_h, waits=[tok_vh, last["s2t"]], inc=sS)

            def v_sh(e):
                vbar(e, e.tensor_scalar(QH[:], QH[:], 127.0, None, OP.mult))
                vbar(e, e.tensor_tensor(BCOL[:], EXL[:],
                                        bass.AP(RSC2, 0, [[1, 128], [0, 2]]), OP.mult))
                return e.tensor_copy(QH[:], QH[:])
            tok_sh = p.step("vector", v_sh, waits=[tok_sh], inc=sV)
            last["sh"] = tok_sh
            yield

            def t_b(e):
                e.transpose(PTR[0:1, 0:128], BCOL[:, 0:1], IDF[:])
                return e.transpose(PTR[0:1, 128:256], BCOL[:, 1:2], IDF[:])

            def t_bb_fn(e):
                return e.matmul(PM1[:, 256:512], ONESF[0:1, :], QR1[0:1, 0:256],
                                start=True, stop=True)

            tok_tb = p.step("tensor", t_b, waits=[tok_sh, last["ptr_free"]], inc=sT)
            tok_vb = p.step("vector", lambda e: e.tensor_copy(QR1[0:1, 0:256], PTR[0:1, 0:256]),
                            waits=[tok_tb], inc=sV)
            last["ptr_free"] = tok_vb

            tok_tbb = p.step("tensor", t_bb_fn, waits=[tok_vb, last["pm1_free"]], inc=sT)
            tok_vbb = p.step("vector", lambda e: e.tensor_copy(BETA[:], PM1[:, 256:512]),
                             waits=[tok_tbb], inc=sV)
            last["pm1_free"] = tok_vbb
            yield

            yield "stg"
            rtok = tok_sh
            for tau in range(NTAU):
                def s_r(e, tau=tau):
                    return e.activation(HTM[tau][:], HTM[tau][:], AF.Identity,
                                        scale=QH[:, tau:tau + 1], bias=C128[:, 0:1])
                tok_sr = p.step("scalar", s_r, waits=[rtok, gelu_tok], inc=sS)
                pw = last["pstg_free"] if tau == 0 else None

                def v_r(e, tau=tau):
                    for hh in range(2):
                        vp = HTM[tau][:, hh * 2048:(hh + 1) * 2048]
                        ci = CINT[:, 0:2048]
                        gb = flat2(LNB, 2048, NG * TT)
                        e.tensor_copy(ci, vp)
                        e.scalar_tensor_tensor(gb, vp, -0.5, ci, OP.add, OP.is_gt)
                        last_i = e.scalar_tensor_tensor(
                            PSTG[:, tau, hh * 2048:(hh + 1) * 2048], ci, -128.0, gb,
                            OP.add, OP.add)
                    return last_i
                rtok = p.step("vector", v_r, waits=[tok_sr, pw], inc=sV)
                yield
            if stage <= 4:
                res["tok"] = rtok
                return

            yield "mm2"
            w2tok = last["w2s"]
            mm2_tok = None
            evh_toks = [None] * NH
            wd_toks = [None] * NH

            def emit_trh(m):
                mi = m % 2

                def t_trh(e, m=m, mi=mi):
                    for tau in range(NTAU):
                        last_i = e.transpose(
                            PTB[:, mi * 512 + tau * 128:mi * 512 + (tau + 1) * 128],
                            PSTG[:, tau, m * 128:(m + 1) * 128], IDB[:])
                    return last_i
                tok_trh = p.step("tensor", t_trh, waits=[rtok, last["ptb_free"]], inc=sT)
                if m == NH - 1:
                    last["pstg_free"] = tok_trh

                def v_evh(e, mi=mi):
                    return e.copy(XQHF[:, mi, :],
                                  PTB[:, mi * 512:mi * 512 + TT])
                tok_evh = p.step("scalar", v_evh, waits=[tok_trh], inc=sS)
                last["ptb_free"] = tok_evh
                evh_toks[m] = tok_evh

            def emit_mm2(m):
                nonlocal mm2_tok
                mi = m % 2
                wb = m % 2

                def t_mm2(e, m=m, mi=mi, wb=wb):
                    for d in range(NG):
                        last_i = e.matmul(PKVR[:, d * TT:(d + 1) * TT],
                                          W2S[wb][:, d * 128:(d + 1) * 128],
                                          XQHF[:, mi, :],
                                          start=(m == 0 and d % 2 == 0),
                                          stop=(m == NH - 1 and d % 2 == 1))
                    return last_i
                mw = [evh_toks[m], wd_toks[m]]
                if m == 0:
                    mw.append(last["pkvr_free"])
                mm2_tok = p.step("tensor", t_mm2, waits=mw, inc=sT)
                w2tok[m % 2] = mm2_tok

            if stage <= 4.5:
                for m in range(NH):
                    emit_trh(m)
                mm2_tok = evh_toks[-1]
            else:
                for m in range(NH):
                    wd_toks[m] = p.dma("sync", [(W2S[m % 2][:], w2t[:, m, :])], sW,
                                       waits=[w2tok[m % 2], last["w2t"]])
                    emit_trh(m)
                    if m >= 1:
                        emit_mm2(m - 1)
                    if m % 8 == 7:
                        yield
                emit_mm2(NH - 1)
            last["w2s"] = w2tok
            if stage <= 4.5:
                res["tok"] = mm2_tok
                return
            if stage <= 4.8:
                tok_dummy = p.step("vector",
                                   lambda e: e.tensor_copy(ABSW[:], v3(PKVR, 2048)),
                                   waits=[mm2_tok], inc=sV)
                last["pkvr_free"] = tok_dummy
                res["tok"] = tok_dummy
                return
            if stage <= 4.9:
                _tb = p.step("tensor", t_b, waits=[tok_sh, last["ptr_free"]], inc=sT)
                _vb = p.step("vector", lambda e: e.tensor_copy(QR1[0:1, 0:256], PTR[0:1, 0:256]),
                             waits=[_tb], inc=sV)
                last["ptr_free"] = _vb
                _tbb = p.step("tensor", t_bb_fn, waits=[_vb, last["pm1_free"]], inc=sT)
                _vbb = p.step("vector", lambda e: e.tensor_copy(BETA[:], PM1[:, 256:512]),
                              waits=[_tbb], inc=sV)
                last["pm1_free"] = _vbb
                tok_dummy = p.step("vector",
                                   lambda e: e.tensor_copy(ABSW[:], v3(PKVR, 2048)),
                                   waits=[mm2_tok, _vbb], inc=sV)
                last["pkvr_free"] = tok_dummy
                res["tok"] = tok_dummy
                return

            def v_epi(e):
                pk = v3(PKVR, 2048)
                e.tensor_tensor(ABSW[:], pk, bc_row(BETA), OP.mult)
                for g in range(NG):
                    last_i = e.scalar_tensor_tensor(xv[:, g, :], ABSW[:, g, :],
                                                    B2H[:, g:g + 1], xv[:, g, :],
                                                    OP.add, OP.add)
                return last_i
            tok_epi = p.step("vector", v_epi, waits=[mm2_tok, tok_vbb, xtok], inc=sV)
            last["pkvr_free"] = tok_epi
            res["tok"] = tok_epi

        def gen_attn(c, nb, xtok, ln2tok_c, ln2tok_n, res):
            xv = XB[nb]
            lv = LN2[c % 3]
            lnx = LN2[(c + 1) % 3]
            TMP1f = KT
            TMP2f = VT

            def v_mix(e):
                e.tensor_tensor(TMP1f[:], lv[:], bc_col(AM), OP.mult)
                e.tensor_tensor(TMP2f[:, :, 1:TT], lv[:, :, 0:TT - 1],
                                bc_col(BM, inner=TT - 1), OP.mult)
                e.tensor_tensor(TMP2f[:, :, 0:1], HAL[:], bc_col(BM, inner=1), OP.mult)
                for g in range(4, NG):
                    e.scalar_tensor_tensor(TMP1f[:, g, 0:TT - 1], lv[:, g, 1:TT],
                                           CM8[:, g:g + 1], TMP1f[:, g, 0:TT - 1],
                                           OP.mult, OP.add)
                    nxt = lnx[:, g, 0:1] if c < NCH - 1 else ZCOL[:, g, :]
                    e.scalar_tensor_tensor(TMP1f[:, g, TT - 1:TT], nxt,
                                           CM8[:, g:g + 1], TMP1f[:, g, TT - 1:TT],
                                           OP.mult, OP.add)
                return e.tensor_tensor(XM[:], TMP1f[:], TMP2f[:], OP.add)
            tok_mix = p.step("vector", v_mix,
                             waits=[ln2tok_c, ln2tok_n, last["scan"]], inc=sV)
            tok_hal = p.step("vector", lambda e: e.tensor_copy(HAL[:], lv[:, :, TT - 1:TT]),
                             waits=[tok_mix], inc=sV)
            yield

            wkstok = last["wks"]
            evac_k = None
            evac_v = None
            prev_mm = None
            for pi, wsrc in enumerate([wkt, wvt, wrt]):
                for j in range(NG):
                    wb = (pi * NG + j) % 3
                    tok_wd = p.dma("sync", [(WKS[wb][:], wsrc[:, j, :])], sW,
                                   waits=[wkstok[wb], last["wkvrt"]])

                    def t_kvr(e, j=j, wb=wb):
                        for i in range(NG):
                            last_i = e.matmul(PKVR[:, i * TT:(i + 1) * TT],
                                              WKS[wb][:, i * 128:(i + 1) * 128],
                                              XM[:, j, :],
                                              start=(j == 0 and i % 2 == 0),
                                              stop=(j == NG - 1 and i % 2 == 1))
                        return last_i
                    mw = [tok_wd, tok_mix, prev_mm]
                    if pi == 0 and j == 0:
                        mw.append(last["pkvr_free"])
                    if pi == 1 and j == 0:
                        mw.append(evac_k)
                    if pi == 2 and j == 0:
                        mw.append(evac_v)
                    prev_mm = p.step("tensor", t_kvr, waits=mw, inc=sT)
                    wkstok[wb] = prev_mm
                if pi == 0:
                    evac_k = p.step("vector",
                                    lambda e: e.tensor_scalar_min(KT[:], v3(PKVR, 2048),
                                                                  KCLAMP),
                                    waits=[prev_mm], inc=sV)
                elif pi == 1:
                    evac_v = p.step("scalar", lambda e: e.copy(VT[:], v3(PKVR, 2048)),
                                    waits=[prev_mm], inc=sS)
                yield
            tok_rmm = prev_mm
            last["wks"] = wkstok

            tok_ke = p.step("scalar", lambda e: e.activation(KT[:], KT[:], AF.Exp),
                            waits=[evac_k], inc=sS)
            yield


            half_y = []
            sg_toks = []
            prev_half = tok_ke
            for hh in range(2):
                g0 = hh * 4
                KTh = KT[:, g0:g0 + 4, :]
                VTh = VT[:, g0:g0 + 4, :]
                rph = bass.AP(PKVR, g0 * TT, [[2048, 128], [TT, 4], [1, TT]])

                def v_mscan(e, g0=g0, KTh=KTh):
                    e.tensor_copy(MT[:, :, 0:1], CARM[:, g0:g0 + 4])
                    for g in range(4):
                        last_i = e.tensor_tensor_scan(MT[:, g, 1:TT + 1], KTh[:, g, :],
                                                      KTh[:, g, :], CARM[:, g0 + g:g0 + g + 1],
                                                      OP.max, OP.max)
                    return last_i
                tok_ms = p.step("vector", v_mscan, waits=[tok_ke, last["scan"], prev_half],
                                inc=sV)

                tok_g1 = p.step("vector",
                                lambda e, KTh=KTh: e.tensor_tensor(GT[:], KTh,
                                                                   MT[:, :, 1:TT + 1],
                                                                   OP.subtract),
                                waits=[tok_ms], inc=sV)
                tok_g2 = p.step("scalar", lambda e: e.activation(GT[:], GT[:], AF.Exp),
                                waits=[tok_g1], inc=sS)
                yield

                def v_lam(e, g0=g0):
                    for g in range(4):
                        last_i = e.scalar_tensor_tensor(TMP1[:, g, :], MT[:, g, 0:TT],
                                                        W8[:, g0 + g:g0 + g + 1],
                                                        MT[:, g, 1:TT + 1],
                                                        OP.add, OP.subtract)
                    return last_i
                tok_l1 = p.step("vector", v_lam, waits=[tok_ms], inc=sV)
                tok_l2 = p.step("scalar", lambda e: e.activation(TMP1[:], TMP1[:], AF.Exp),
                                waits=[tok_l1], inc=sS)
                yield

                def v_scans(e, g0=g0, KTh=KTh, VTh=VTh):
                    e.tensor_tensor(KTh, GT[:], VTh, OP.mult)
                    e.tensor_copy(AT[:, :, 0:1], CARA[:, g0:g0 + 4])
                    e.tensor_copy(BT[:, :, 0:1], CARB[:, g0:g0 + 4])
                    for g in range(4):
                        e.tensor_tensor_scan(AT[:, g, 1:TT + 1], TMP1[:, g, :], KTh[:, g, :],
                                             CARA[:, g0 + g:g0 + g + 1], OP.mult, OP.add)
                    for g in range(4):
                        last_i = e.tensor_tensor_scan(BT[:, g, 1:TT + 1], TMP1[:, g, :],
                                                      GT[:, g, :],
                                                      CARB[:, g0 + g:g0 + g + 1],
                                                      OP.mult, OP.add)
                    return last_i
                tok_sc = p.step("vector", v_scans, waits=[tok_l2, tok_g2], inc=sV)
                yield

                def v_post1(e, g0=g0, VTh=VTh):
                    for g in range(4):
                        e.tensor_scalar(GT[:, g, :], GT[:, g, :],
                                        U8E[:, g0 + g:g0 + g + 1], None, OP.mult)
                    e.tensor_tensor(VTh, GT[:], VTh, OP.mult)
                    for g in range(4):
                        e.tensor_scalar(TMP2[:, g, :], TMP1[:, g, :],
                                        EWN[:, g0 + g:g0 + g + 1], None, OP.mult)
                    e.tensor_tensor(TMP1[:], AT[:, :, 0:TT], TMP2[:], OP.mult)
                    e.tensor_tensor(TMP2[:], BT[:, :, 0:TT], TMP2[:], OP.mult)
                    e.tensor_tensor(TMP2[:], TMP2[:], GT[:], OP.add)
                    return e.tensor_tensor(TMP1[:], TMP1[:], VTh, OP.add)
                tok_p1 = p.step("vector", v_post1, waits=[tok_sc], inc=sV)

                tok_sgh = p.step("scalar",
                                 lambda e, rph=rph: e.activation(GT[:], rph, AF.Exp,
                                                                 scale=-1.0),
                                 waits=[tok_rmm, evac_v, tok_p1], inc=sS)
                tok_sg1 = p.step("vector",
                                 lambda e: e.tensor_scalar(GT[:], GT[:], 1.0, None, OP.add),
                                 waits=[tok_sgh], inc=sV)
                sg_toks.append(tok_sg1)
                tok_dn = p.step("vector",
                                lambda e: e.tensor_tensor(TMP2[:], TMP2[:], GT[:],
                                                          OP.mult),
                                waits=[tok_sg1], inc=sV)
                yield

                def s_rec(e):
                    e.activation(TMP2[:], TMP2[:], AF.Ln)
                    return e.activation(TMP2[:], TMP2[:], AF.Exp, scale=-1.0)
                tok_rc = p.step("scalar", s_rec, waits=[tok_dn], inc=sS)

                def v_y(e, hh=hh, g0=g0):
                    e.tensor_tensor(YB[:, g0:g0 + 4, :], TMP1[:], TMP2[:], OP.mult)
                    e.tensor_copy(CARA[:, g0:g0 + 4], AT[:, :, TT:TT + 1])
                    e.tensor_copy(CARB[:, g0:g0 + 4], BT[:, :, TT:TT + 1])
                    return e.tensor_copy(CARM[:, g0:g0 + 4], MT[:, :, TT:TT + 1])
                tok_y = p.step("vector", v_y, waits=[tok_rc], inc=sV)
                half_y.append(tok_y)
                prev_half = tok_y
                yield
            last["scan"] = half_y[-1]
            tok_y = half_y[-1]

            prev_mm = None
            for j in range(NG):
                wb = j % 3
                tok_wd = p.dma("sync", [(WKS[wb][:], wot[:, j, :])], sW,
                               waits=[wkstok[wb], last["wot"]])

                def t_wo(e, j=j, wb=wb):
                    for i in range(NG):
                        last_i = e.matmul(PKVR[:, i * TT:(i + 1) * TT],
                                          WKS[wb][:, i * 128:(i + 1) * 128],
                                          YB[:, j, :],
                                          start=(j == 0 and i % 2 == 0),
                                          stop=(j == NG - 1 and i % 2 == 1))
                    return last_i
                mw = [tok_wd, half_y[0] if j < 4 else half_y[1], prev_mm]
                if j == 0:
                    mw.extend(sg_toks)
                prev_mm = p.step("tensor", t_wo, waits=mw, inc=sT)
                wkstok[wb] = prev_mm
            yield

            tok_res = p.step("vector",
                             lambda e: e.tensor_tensor(xv[:], xv[:], v3(PKVR, 2048), OP.add),
                             waits=[prev_mm, xtok], inc=sV)
            last["pkvr_free"] = tok_res
            res["tok"] = tok_res

        def emit_out(c, nb, xtok):
            tok_o = None
            for tau in range(NTAU):
                tok_e = None
                for gq in range(2):
                    def t_to(e, tau=tau, gq=gq, nb=nb):
                        for gg in range(4):
                            g = gq * 4 + gg
                            last_i = e.transpose(PTR[:, gg * 128:(gg + 1) * 128],
                                                 XB[nb][:, g, tau * 128:(tau + 1) * 128],
                                                 IDF[:])
                        return last_i
                    tok_t = p.step("tensor", t_to,
                                   waits=[xtok, last["ptr_free"], last["tsg_free"], tok_o],
                                   inc=sT)
                    tok_e = p.step("scalar",
                                   lambda e, gq=gq: e.copy(
                                       TSG[:, gq * 512:(gq + 1) * 512], PTR[:, 0:512]),
                                   waits=[tok_t], inc=sS)
                    last["ptr_free"] = tok_e
                tok_o = p.dma("sync", [(y_out[c * TT + tau * 128: c * TT + (tau + 1) * 128, :],
                                        TSG[:, :])], sO, waits=[tok_e])
                last["tsg_free"] = tok_o
            last["xb"][nb] = tok_o
            return tok_o

        x1tok = [None] * NCH
        ln2tok = [None] * NCH

        _STAGE = {"f_ln": 1, "f_q": 2, "f_mm1": 3, "f_hq": 4, "f_tr": 4.5,
                  "f_mm2": 4.8, "f_bb": 4.9}

        def gen_front(c):
            tok_in = emit_in(c)
            nb = c % 3
            yield "x0"
            if parts in ("full", "ffn1", "ffn1ln", "dbg_ffn1"):
                r = {}
                yield from gen_ffn(c, nb, tok_in, r)
                t1 = r["tok"]
            elif parts in _STAGE:
                r = {}
                yield from gen_ffn(c, nb, tok_in, r, stage=_STAGE[parts])
                t1 = r["tok"]
            else:
                t1 = tok_in
            x1tok[c] = t1
            if parts in ("full", "ffn1ln"):
                ln2tok[c] = emit_ln(c, XB[nb], LN2[c % 3], t1, use_pkvr=True)
            else:
                ln2tok[c] = t1

        def drive(g):
            for _ in g:
                pass

        def interleave(a, b):
            alive_a, alive_b, b_paused = True, True, False
            while alive_a or alive_b:
                if alive_a:
                    try:
                        next(a)
                    except StopIteration:
                        alive_a = False
                        if b_paused:
                            b_paused = False
                if alive_b and not b_paused:
                    try:
                        if next(b) == "mm2" and alive_a:
                            b_paused = True
                    except StopIteration:
                        alive_b = False

        def interleave_hold(a, b):
            alive_b, b_paused = b is not None, False
            while True:
                try:
                    next(a)
                except StopIteration:
                    break
                if alive_b and not b_paused:
                    try:
                        if next(b) == "mm2":
                            b_paused = True
                    except StopIteration:
                        alive_b = False
            return b if alive_b else None

        def gen_tail(c, nb, atok):
            rf = {}
            yield from gen_ffn(c, nb, atok, rf)
            emit_out(c, nb, rf["tok"])

        PRO_MARK = {"s1": "s1_done", "w1": "w1_done", "s2": "s2_done",
                    "stg": "stg_done", "mm2": "stg_done"}

        fronts = {}
        if parts == "full" and PUMP_FRONTS:
            fronts = {ci: gen_front(ci) for ci in range(min(N_PUMP, NCH))}
            flags = set()
            paused = {}
            P = gen_prologue()
            while True:
                try:
                    mk = next(P)
                except StopIteration:
                    break
                if mk:
                    flags.add(mk)
                for ci in fronts:
                    g = fronts[ci]
                    if paused.get(ci) and paused[ci] not in flags:
                        continue
                    paused[ci] = None
                    try:
                        v = next(g)
                    except StopIteration:
                        continue
                    if ci == 0:
                        req = ("stg_done" if v in F0_HOLD else PRO_MARK.get(v)) \
                            if v is not None else None
                    else:
                        req = "stg_done" if v is not None else None
                    if req and req not in flags:
                        paused[ci] = req
            drive(fronts.pop(0))
            if 1 in fronts:
                drive(fronts.pop(1))
            else:
                drive(gen_front(1))
        else:
            g0 = gen_front(0)
            if EARLY_IN0:
                next(g0)
            drive(gen_prologue())
            drive(g0)
            if NCH > 1:
                drive(gen_front(1))

        for c in range(NCH):
            nb = c % 3
            if parts == "full":
                ra = {}
                A = gen_attn(c, nb, x1tok[c], ln2tok[c],
                             ln2tok[c + 1] if c + 1 < NCH else ln2tok[c], ra)
                B = fronts.pop(c + 2, None)
                if B is None and c + 2 < NCH:
                    B = gen_front(c + 2)
                B = interleave_hold(A, B)
                F = gen_tail(c, nb, ra["tok"])
                if B is not None:
                    interleave(B, F)
                else:
                    drive(F)
            else:
                t3 = x1tok[c]
                if c + 2 < NCH:
                    drive(gen_front(c + 2))
                emit_out(c, nb, t3)

        if parts.startswith("dbg"):
            dbg_outs = []
            def dump(name, ap, shape, dt=F32):
                o = nc.dram_tensor("dbg_" + name, shape, dt, kind="ExternalOutput").ap()
                dbg_outs.append((o, ap))
            dump("xqy", XQY[:], [128, NG, TT], BF16)
            dump("htm0", HTM[0][:], [128, H])
            dump("acol", ACOL[:], [128, 2])
            dump("qcol", QCOL[:], [128, 2])
            dump("xsc", XSC[:], [128, 2])
            dump("hsc", HSC[:], [128, 2])
            dump("qh", QH[:], [128, 2])
            dump("rsbeta", RS[:], [128, TT])
            dump("lnb", LNB[:], [128, NG, TT])
            dump("pstg0", PSTG[:, 0, :], [128, H], BF16)
            dump("w1s0", W1S[0][:], [128, NG, 512], BF16)
            dump("ls1", LS1[:], [128, 1])
            dump("rs1", RS1[:], [128, 1])
            dump("irb", IRB[0:1, 0, 0:256], [1, 256], BF16)
            dump("absw", ABSW[:], [128, NG, TT])
            tok_dbg = p.dma("gpsimd", [(o, ap) for (o, ap) in dbg_outs], sI,
                            waits=[last["scan"], last["pkvr_free"], last["xb"][0],
                                   last["xb"][1]])
            p.streams["gpsimd"].append(lambda e: e.wait_ge(sI.h, sI.n))
        final_o = sO.n
        p.streams["sync"].append(lambda e: e.wait_ge(sO.h, final_o))

        with nc.Block() as block:
            block.sync(lambda e: [fn(e) for fn in p.streams["sync"]])
            block.tensor(lambda e: [fn(e) for fn in p.streams["tensor"]])
            block.vector(lambda e: [fn(e) for fn in p.streams["vector"]])
            block.scalar(lambda e: [fn(e) for fn in p.streams["scalar"]])
            block.gpsimd(lambda e: [fn(e) for fn in p.streams["gpsimd"]])

    return nc


def kernel(L0_attn_x, L0_attn_time_decay, L0_attn_time_first, L0_attn_time_mix,
           L0_attn_combined_mix, L0_attn_Wk, L0_attn_Wv, L0_attn_Wr, L0_attn_Wo,
           L0_ffn_W1, L0_ffn_b1, L0_ffn_W2, L0_ffn_b2):
    x = np.ascontiguousarray(np.asarray(L0_attn_x, dtype=np.float32))
    args = {
        "td": np.ascontiguousarray(np.asarray(L0_attn_time_decay, np.float32)),
        "tf": np.ascontiguousarray(np.asarray(L0_attn_time_first, np.float32)),
        "tm": np.ascontiguousarray(np.asarray(L0_attn_time_mix, np.float32)),
        "cm": np.ascontiguousarray(np.asarray(L0_attn_combined_mix, np.float32)),
        "wk": np.ascontiguousarray(np.asarray(L0_attn_Wk, np.float32)),
        "wv": np.ascontiguousarray(np.asarray(L0_attn_Wv, np.float32)),
        "wr": np.ascontiguousarray(np.asarray(L0_attn_Wr, np.float32)),
        "wo": np.ascontiguousarray(np.asarray(L0_attn_Wo, np.float32)),
        "w1": np.ascontiguousarray(np.asarray(L0_ffn_W1, np.float32)),
        "b1": np.ascontiguousarray(np.asarray(L0_ffn_b1, np.float32)),
        "w2": np.ascontiguousarray(np.asarray(L0_ffn_W2, np.float32)),
        "b2": np.ascontiguousarray(np.asarray(L0_ffn_b2, np.float32)),
    }
    nc = build_program(PARTS)
    in_maps = [dict(args, x=np.ascontiguousarray(x[b])) for b in range(8)]
    res = run_bass_kernel_spmd(nc, in_maps, core_ids=list(range(8)))
    return np.stack([np.asarray(r["y"]) for r in res.results], axis=0)
